# version 01
# baseline (speedup 1.0000x reference)
import numpy as np
from contextlib import ExitStack
import concourse.bass as bass
import concourse.mybir as mybir
from concourse.bass_utils import run_bass_kernel_spmd

F32 = mybir.dt.float32
BF16 = mybir.dt.bfloat16
AF = mybir.ActivationFunctionType
ALU = mybir.AluOpType
AX = mybir.AxisListType

T = 4096
D = 1024
DIN = 2048
NST = 16
DTR = 64
NCORES = 8
TT = 512
NTT = T // TT
NDG = DIN // 128
NXC = D // 128

ENGINES = ("sync", "tensor", "scalar", "vector", "gpsimd")


class Prog:

    def __init__(self, nc, ctx):
        self.nc = nc
        self.ctx = ctx
        self.v = {}
        self.sems = {}
        for name in ("pe", "act", "dve", "gp", "bar"):
            self.sems[name] = ctx.enter_context(nc.semaphore("s_" + name))
            self.v[name] = 0

    def sem(self, name):
        if name not in self.sems:
            self.sems[name] = self.ctx.enter_context(self.nc.semaphore(name))
            self.v[name] = 0
        return self.sems[name]


class Plan:

    def __init__(self, p):
        self.p = p
        self.items = {e: [] for e in ENGINES}

    def op(self, eng, emit, waits=(), inc=None, inck=1, tag=""):
        p = self.p
        wlist = [(p.sem(s), v) for (s, v) in waits if v > 0]
        tgt = None
        if inc is not None:
            p.sem(inc)
            p.v[inc] += inck
            tgt = (p.sems[inc], inck)

        def run(h, emit=emit, wlist=wlist, tgt=tgt):
            for sem, val in wlist:
                h.wait_ge(sem, val)
            ins = emit(h)
            if tgt is not None:
                ins.then_inc(*tgt)

        run.meta = (eng, tuple((s, v) for (s, v) in waits if v > 0),
                    (inc, inck) if inc else None, tag)
        self.items[eng].append(run)
        return p.v[inc] if inc is not None else None

    def barrier(self):
        p = self.p
        tgt = p.v["bar"] + len(ENGINES)
        dwaits = [(n, p.v[n]) for n in list(p.v) if n.startswith("d_")]
        pre = {"sync": dwaits}
        for e in ENGINES:
            self.op(e, lambda h: h.nop(), waits=pre.get(e, ()), inc="bar")
        for e in ENGINES:
            self.items[e].append(lambda h, t=tgt, s=p.sems["bar"]: h.wait_ge(s, t))
        p.v["bar"] = tgt

    def realize(self, nc):
        with nc.Block() as blk:
            for eng in ENGINES:
                fns = self.items[eng]
                if fns:
                    getattr(blk, eng)(lambda h, fns=fns: [f(h) for f in fns])


def check_plans(plans):
    items = {e: [] for e in ENGINES}
    for pl in plans:
        for e in ENGINES:
            items[e].extend(pl.items[e])
    heads = {e: 0 for e in ENGINES}
    vals = {}
    progress = True
    while progress:
        progress = False
        for e in ENGINES:
            while heads[e] < len(items[e]):
                fn = items[e][heads[e]]
                meta = getattr(fn, "meta", None)
                if meta is None:
                    heads[e] += 1
                    progress = True
                    continue
                _, waits, inc, tag = meta
                if all(vals.get(s, 0) >= v for (s, v) in waits):
                    if inc:
                        vals[inc[0]] = vals.get(inc[0], 0) + inc[1]
                    heads[e] += 1
                    progress = True
                else:
                    break
    stuck = {e: heads[e] for e in ENGINES if heads[e] < len(items[e])}
    if stuck:
        msg = ["PLAN DEADLOCK:"]
        for e, i in stuck.items():
            meta = getattr(items[e][i], "meta", None)
            msg.append(f"  {e}[{i}/{len(items[e])}] meta={meta} vals_now=" +
                       str({s: vals.get(s, 0) for (s, v) in (meta[1] if meta else ())}))
        raise RuntimeError("\n".join(msg))


def make_ident(pl, ident_f32, ident_bf, extra_consts=()):
    def go(g):
        for ident in (ident_f32, ident_bf):
            g.memset(ident[:], 0.0)
            i = g.affine_select(
                out=ident[:], in_=ident[:], compare_op=ALU.not_equal,
                fill=1.0, base=0, pattern=[[-1, 128]], channel_multiplier=1,
            )
        for ap, val in extra_consts:
            i = g.memset(ap, val)
        return i
    return pl.op("gpsimd", go, inc="gp")


def prep_lhsT(pl, sb, ps, w_dram, M, K, out_tiles, ident_f32, name, pst, ev_of_bank,
              wld=None, tr_state=None):
    p = pl.p
    nkc = K // 128
    nmc = (M + 127) // 128
    if wld is None:
        wld = [sb(f"{name}_wld{i}", [min(M, 128), K], F32) for i in range(2)]
    tr_of_buf = tr_state if tr_state is not None else [0, 0]
    for mc in range(nmc):
        mrows = min(128, M - mc * 128)
        buf = mc % 2
        ld = pl.op("sync",
                   lambda s, mc=mc, buf=buf, mrows=mrows: s.dma_start(
                       out=wld[buf][0:mrows, 0:K], in_=w_dram[mc * 128:mc * 128 + mrows, :]),
                   waits=[("pe", tr_of_buf[buf])], inc=f"d_{name}w{buf}", inck=16)
        for kc in range(nkc):
            pb = kc % 2
            tr = pl.op("tensor",
                       lambda t, kc=kc, pb=pb, buf=buf, mrows=mrows: t.transpose(
                           pst[pb][:, 0:mrows], wld[buf][0:mrows, kc * 128:(kc + 1) * 128],
                           ident_f32[0:mrows, 0:mrows]),
                       waits=[(f"d_{name}w{buf}", ld), ("act", ev_of_bank[pb])], inc="pe")
            tr_of_buf[buf] = tr
            ev = pl.op("scalar",
                       lambda a, kc=kc, pb=pb, mc=mc, mrows=mrows: a.activation(
                           out_tiles[kc][:, mc * 128:mc * 128 + mrows], pst[pb][:, 0:mrows], AF.Copy),
                       waits=[("pe", tr)], inc="act")
            ev_of_bank[pb] = ev


def col_load(pl, dst_col, src_vec_slice, sem="d_cst"):
    return pl.op("sync",
                 lambda s: s.dma_start(out=dst_col, in_=src_vec_slice.unsqueeze(1)),
                 inc=sem, inck=16)


def emit_a1a(p, pl, sb, ps, io):
    ident_f32 = sb("idf", [128, 128], F32)
    ident_bf = sb("idb", [128, 128], BF16)
    eps_c = sb("eps_c", [128, 1], F32)
    idv = make_ident(pl, ident_f32, ident_bf, [(eps_c[:], 1e-5)])

    inT = [sb(f"inT{kc}", [128, 4096], BF16) for kc in range(NXC)]
    xpT = [sb(f"xpT{dg}", [128, 96], BF16) for dg in range(NDG)]
    pst = [ps(f"pstw{i}", [128, 128], F32) for i in range(2)]
    evb = [0, 0]
    prep_lhsT(pl, sb, ps, io["in_proj_w"], 4096, D, inT, ident_f32, "ip", pst, evb)
    prep_lhsT(pl, sb, ps, io["x_proj_w"], 96, DIN, xpT, ident_f32, "xp", pst, evb)

    convw = sb("convw", [128, 4 * NDG], F32)
    convb = sb("convb", [128, NDG], F32)
    pl.op("sync", lambda s: s.dma_start(
        out=convw[:].rearrange("p (g k) -> p g k", k=4),
        in_=io["conv_w"].rearrange("(g p) o k -> p g (o k)", p=128)),
        inc="d_cstA", inck=16)
    for dg in range(NDG):
        ld_const = col_load(pl, convb[:, dg:dg + 1], io["conv_b"][dg * 128:(dg + 1) * 128],
                            sem="d_cstA")

    xt = [sb(f"xt{i}", [128, D], F32) for i in range(8)]
    sq = sb("sq", [128, D], F32)
    stat = sb("stat", [128, 8], F32)
    xn_bf = [sb(f"xnbf{i}", [128, D], BF16) for i in range(2)]
    xn_fm = [sb(f"xnfm{kc}", [128, TT], BF16) for kc in range(NXC)]
    p_tr = [ps(f"ptr{i}", [128, 128], BF16) for i in range(2)]
    p_mm = [ps(f"pmm{i}", [128, TT], F32) for i in range(2)]
    p_xp = ps("pxp", [96, TT], F32)
    p_ct = ps("pct", [128, 16], BF16)
    xm_t = sb("xm_t", [128, 3 + TT], F32)
    halo = sb("halo", [128, 3 * NDG], F32)
    u_acc = sb("u_acc", [128, TT], F32)
    u_dg = [sb(f"u_dg{i}", [128, TT], F32) for i in range(2)]
    res_bf = [sb(f"res_bf{i}", [128, TT], BF16) for i in range(2)]
    ub_all = [sb(f"ub{dg}", [128, TT], BF16) for dg in range(NDG)]
    dtbc_bf = sb("dtbc_bf", [64, TT], BF16)
    bc_bf = sb("bc_bf", [32, TT], BF16)
    c_bf = sb("c_bf", [16, TT], BF16)
    ct_bf = sb("ct_bf", [128, 16], BF16)

    halo_zero = pl.op("gpsimd", lambda g: g.memset(halo[:], 0.0), inc="gp")

    xt_done = [0] * 8
    ln_prev = {"a3": 0, "g5": 0, "a6": 0, "d7": 0, "d9": 0}
    xnbf_free = [0, 0]
    ptr_ev = [0, 0]
    pmm_ev = [max(evb), max(evb)]
    xmt_free = 0
    uacc_free = 0
    u_st = [0, 0]
    res_st = [0, 0]
    pxp_ev = max(evb)
    dt_st = 0
    bc_st = 0
    cbf_free = 0
    cbf_ld = 0
    ct_st = 0
    pct_ev = 0

    for tt in range(NTT):
        ldx = {}
        for pi in range(4):
            buf = (tt % 2) * 4 + pi
            ldx[pi] = pl.op("sync",
                            lambda s, tt=tt, pi=pi, buf=buf: s.dma_start(
                                out=xt[buf][:], in_=io["x"][(tt * 4 + pi) * 128:(tt * 4 + pi + 1) * 128, :]),
                            waits=[("dve", xt_done[buf])], inc=f"d_xt{buf}", inck=16)

        for pi in range(4):
            buf = (tt % 2) * 4 + pi
            nb = pi % 2
            d1 = pl.op("vector", lambda v, buf=buf: v.reduce_sum(stat[:, 0:1], xt[buf][:], AX.X),
                       waits=[(f"d_xt{buf}", ldx[pi]), ("act", ln_prev["a3"])], inc="dve")
            a2 = pl.op("scalar",
                       lambda a, buf=buf: a.activation(sq[:], xt[buf][:], AF.Square,
                                                       accum_out=stat[:, 1:2]),
                       waits=[(f"d_xt{buf}", ldx[pi]), ("gp", ln_prev["g5"])], inc="act")
            a3 = pl.op("scalar",
                       lambda a: a.activation(stat[:, 2:3], stat[:, 0:1], AF.Copy, scale=1.0 / D),
                       waits=[("dve", max(d1, ln_prev["d9"]))], inc="act")
            d4 = pl.op("vector", lambda v: v.tensor_mul(stat[:, 4:5], stat[:, 2:3], stat[:, 2:3]),
                       waits=[("act", a3), ("gp", ln_prev["g5"])], inc="dve")
            def g5f(g):
                g.tensor_scalar_mul(stat[:, 5:6], stat[:, 1:2], 1.0 / D)
                return g.tensor_sub(stat[:, 5:6], stat[:, 5:6], stat[:, 4:5])
            g5 = pl.op("gpsimd", g5f,
                       waits=[("act", max(a2, ln_prev["a6"])), ("dve", d4)], inc="gp")
            a6 = pl.op("scalar",
                       lambda a: a.activation(stat[:, 6:7], stat[:, 5:6], AF.Ln, bias=eps_c[:]),
                       waits=[("gp", max(g5, idv)), ("dve", ln_prev["d7"])], inc="act")
            d7 = pl.op("vector", lambda v: v.tensor_copy(stat[:, 3:4], stat[:, 6:7]),
                       waits=[("act", a6)], inc="dve")
            a8 = pl.op("scalar",
                       lambda a: a.activation(stat[:, 7:8], stat[:, 3:4], AF.Exp, scale=-0.5),
                       waits=[("dve", d7)], inc="act")
            d9 = pl.op("vector",
                       lambda v, buf=buf, nb=nb: v.tensor_scalar(
                           xn_bf[nb][:], xt[buf][:], stat[:, 2:3], stat[:, 7:8],
                           ALU.subtract, ALU.mult),
                       waits=[("act", a8), ("pe", xnbf_free[nb])], inc="dve")
            ln_prev = {"a3": a3, "g5": g5, "a6": a6, "d7": d7, "d9": d9}
            xn = d9
            xt_done[buf] = xn

            for kc in range(NXC):
                pb = kc % 2
                tr = pl.op("tensor",
                           lambda t, nb=nb, kc=kc, pb=pb: t.transpose(
                               p_tr[pb][:], xn_bf[nb][:, kc * 128:(kc + 1) * 128], ident_bf[:]),
                           waits=[("dve", max(xn, ptr_ev[pb])), ("gp", idv)], inc="pe")
                xnbf_free[nb] = tr
                ev = pl.op("vector",
                           lambda v, kc=kc, pi=pi, pb=pb: v.tensor_copy(
                               xn_fm[kc][:, pi * 128:(pi + 1) * 128], p_tr[pb][:]),
                           waits=[("pe", tr)], inc="dve")
                ptr_ev[pb] = ev
        xnfm_ready = p.v["dve"]

        for mc in range(32):
            pb = mc % 2

            def ipmm(t, mc=mc, pb=pb):
                for kc in range(NXC):
                    i = t.matmul(p_mm[pb][:], inT[kc][:, mc * 128:(mc + 1) * 128],
                                 xn_fm[kc][:], start=(kc == 0), stop=(kc == NXC - 1))
                return i
            mm = pl.op("tensor", ipmm, waits=[("dve", xnfm_ready), ("act", pmm_ev[pb])], inc="pe")

            if mc < 16:
                dg = mc
                ub = dg % 2
                xm = pl.op("scalar",
                           lambda a, pb=pb: a.activation(xm_t[:, 3:3 + TT], p_mm[pb][:], AF.Copy),
                           waits=[("pe", mm), ("dve", xmt_free)], inc="act")
                pmm_ev[pb] = xm

                def convf(v, dg=dg):
                    v.tensor_copy(xm_t[:, 0:3], halo[:, dg * 3:dg * 3 + 3])
                    v.tensor_scalar_mul(u_acc[:], xm_t[:, 3:3 + TT], convw[:, dg * 4 + 3:dg * 4 + 4])
                    for k in (2, 1, 0):
                        v.scalar_tensor_tensor(u_acc[:], xm_t[:, k:k + TT],
                                               convw[:, dg * 4 + k:dg * 4 + k + 1],
                                               u_acc[:], ALU.mult, ALU.add)
                    return v.tensor_copy(halo[:, dg * 3:dg * 3 + 3], xm_t[:, TT:TT + 3])
                conv = pl.op("vector", convf,
                             waits=[("act", max(xm, uacc_free)), ("gp", halo_zero),
                                    ("d_cstA", ld_const)], inc="dve")
                xmt_free = conv

                u = pl.op("scalar",
                          lambda a, dg=dg, ub=ub: a.activation(
                              u_dg[ub][:], u_acc[:], AF.Silu, bias=convb[:, dg:dg + 1]),
                          waits=[("dve", conv), (f"d_ust{ub}", u_st[ub])], inc="act")
                uacc_free = u

                ubf = pl.op("gpsimd",
                            lambda g, dg=dg, ub=ub: g.tensor_copy(ub_all[dg][:], u_dg[ub][:]),
                            waits=[("act", u)], inc="gp")

                u_st[ub] = pl.op("sync",
                                 lambda s, tt=tt, dg=dg, ub=ub: s.dma_start(
                                     out=io["u_scr"][dg * 128:(dg + 1) * 128, tt * TT:(tt + 1) * TT],
                                     in_=u_dg[ub][:]),
                                 waits=[("act", u), ("gp", ubf)], inc=f"d_ust{ub}", inck=16)
            else:
                dg = mc - 16
                rb = dg % 2
                res = pl.op("scalar",
                            lambda a, pb=pb, rb=rb: a.activation(res_bf[rb][:], p_mm[pb][:], AF.Silu),
                            waits=[("pe", mm), (f"d_rst{rb}", res_st[rb])], inc="act")
                pmm_ev[pb] = res
                res_st[rb] = pl.op("sync",
                                   lambda s, tt=tt, dg=dg, rb=rb: s.dma_start(
                                       out=io["sres_scr"][dg * 128:(dg + 1) * 128, tt * TT:(tt + 1) * TT],
                                       in_=res_bf[rb][:]),
                                   waits=[("act", res)], inc=f"d_rst{rb}", inck=16)

        ub_ready = p.v["gp"]

        def xpf(t):
            for dg in range(NDG):
                i = t.matmul(p_xp[:], xpT[dg][:], ub_all[dg][:],
                             start=(dg == 0), stop=(dg == NDG - 1))
            return i
        xpmm = pl.op("tensor", xpf, waits=[("gp", ub_ready), ("act", pxp_ev)], inc="pe")

        def xpev(a):
            a.activation(dtbc_bf[:], p_xp[0:64, :], AF.Copy)
            return a.activation(bc_bf[:], p_xp[64:96, :], AF.Copy)
        xev = pl.op("scalar", xpev,
                    waits=[("pe", xpmm), ("d_dst", dt_st), ("d_bst", bc_st),
                           ("d_cmv", cbf_ld)], inc="act")
        pxp_ev = xev

        dt_st = pl.op("sync",
                      lambda s, tt=tt: s.dma_start(out=io["dt_scr"][:, tt * TT:(tt + 1) * TT],
                                                   in_=dtbc_bf[:]),
                      waits=[("act", xev)], inc="d_dst", inck=16)
        bc_st = pl.op("sync",
                      lambda s, tt=tt: s.dma_start(out=io["b_scr"][:, tt * TT:(tt + 1) * TT],
                                                   in_=bc_bf[0:16, :]),
                      inc="d_bst", inck=16)
        c_mv = pl.op("sync", lambda s: s.dma_start(out=c_bf[:], in_=bc_bf[16:32, :]),
                     waits=[("pe", cbf_free), ("act", xev)], inc="d_cmv", inck=16)
        cbf_ld = c_mv
        for q in range(4):
            ctr = pl.op("tensor",
                        lambda t, q=q: t.transpose(p_ct[:, :], c_bf[:, q * 128:(q + 1) * 128],
                                                   ident_bf[0:16, 0:16]),
                        waits=[("d_cmv", c_mv), ("dve", pct_ev)], inc="pe")
            cbf_free = ctr
            ctev = pl.op("vector", lambda v: v.tensor_copy(ct_bf[:], p_ct[:, :]),
                         waits=[("pe", ctr), ("d_cts", ct_st)], inc="dve")
            pct_ev = ctev
            ct_st = pl.op("sync",
                          lambda s, tt=tt, q=q: s.dma_start(
                              out=io["ct_scr"][tt * TT + q * 128:tt * TT + (q + 1) * 128, :],
                              in_=ct_bf[:]),
                          waits=[("dve", ctev)], inc="d_cts", inck=16)


def emit_a1b(p, pl, sb, ps, io):
    ident_f32 = sb("idf2", [128, 128], F32)
    ident_bf = sb("idb2", [128, 128], BF16)
    make_ident(pl, ident_f32, ident_bf)

    opT = [sb(f"opT{dg}", [128, D], BF16) for dg in range(NDG)]
    pst2 = [ps(f"pstw2{i}", [128, 128], F32) for i in range(2)]
    evb2 = [0, 0]
    prep_lhsT(pl, sb, ps, io["out_proj_w"], D, DIN, opT, ident_f32, "op", pst2, evb2)

    dtTl = sb("dtTl", [64, DIN], BF16)
    dtld = sb("dtld", [128, 64], F32)
    p_dtt = ps("p_dtt", [64, 128], F32)
    dttr_ev = 0
    for dg in range(NDG):
        ld = pl.op("sync",
                   lambda s, dg=dg: s.dma_start(out=dtld[:], in_=io["dt_proj_w"][dg * 128:(dg + 1) * 128, :]),
                   waits=[("pe", p.v["pe"])], inc="d_dtl", inck=16)
        tr = pl.op("tensor",
                   lambda t: t.transpose(p_dtt[:, :], dtld[:, 0:64], ident_f32[:]),
                   waits=[("d_dtl", ld), ("act", dttr_ev)], inc="pe")
        dttr_ev = pl.op("scalar",
                        lambda a, dg=dg: a.activation(dtTl[:, dg * 128:(dg + 1) * 128], p_dtt[:, :], AF.Copy),
                        waits=[("pe", tr)], inc="act")

    A_s = [sb(f"As{dg}", [128, NST], F32) for dg in range(NDG)]
    Dp_c = sb("Dp_c", [128, NDG], F32)
    dtb_c = sb("dtb_c", [128, NDG], F32)
    for dg in range(NDG):
        pl.op("sync", lambda s, dg=dg: s.dma_start(out=A_s[dg][:],
                                                   in_=io["A_log"][dg * 128:(dg + 1) * 128, :]),
              inc="d_cstB", inck=16)
        col_load(pl, Dp_c[:, dg:dg + 1], io["Dparam"][dg * 128:(dg + 1) * 128], sem="d_cstB")
        ldc = col_load(pl, dtb_c[:, dg:dg + 1], io["dt_proj_b"][dg * 128:(dg + 1) * 128],
                       sem="d_cstB")

    def mkA(a):
        for dg in range(NDG):
            i = a.activation(A_s[dg][:], A_s[dg][:], AF.Exp)
        return i
    aexp = pl.op("scalar", mkA, waits=[("d_cstB", ldc)], inc="act")

    def negA(v):
        for dg in range(NDG):
            i = v.tensor_scalar_mul(A_s[dg][:], A_s[dg][:], -1.0)
        return i
    aneg = pl.op("vector", negA, waits=[("act", aexp)], inc="dve")

    B_NT = sb("B_NT", [128, NST * TT], BF16)
    C_TN = sb("C_TN", [128, NST * TT], BF16)
    dt_in = sb("dt_in", [64, TT], BF16)
    u_in = [sb(f"u_in{i}", [128, TT], F32) for i in range(2)]
    sres_in = [sb(f"sres_in{i}", [128, TT], BF16) for i in range(2)]
    delta_t = sb("delta_t", [128, TT], F32)
    sp1 = sb("sp1", [128, TT], F32)
    sp2 = sb("sp2", [128, TT], F32)
    du_t = sb("du_t", [128, TT], F32)
    a_t = sb("a_t", [128, TT], F32)
    dBu_t = sb("dBu_t", [128, TT], F32)
    h_buf = sb("h_buf", [128, NST * TT], F32)
    hc_buf = sb("hc_buf", [128, NST * TT], F32)
    y_t = sb("y_t", [128, TT], F32)
    y2_t = sb("y2_t", [128, TT], F32)
    prod_all = [sb(f"prod{dg}", [128, TT], BF16) for dg in range(NDG)]
    h_state = sb("h_state", [128, NST * NDG], F32)
    m_t = [sb(f"m_t{i}", [128, TT], F32) for i in range(2)]
    p_d = ps("p_d", [128, TT], F32)
    p_op = [ps(f"p_op{i}", [128, TT], F32) for i in range(2)]

    hs0 = pl.op("gpsimd", lambda g: g.memset(h_state[:], 0.0), inc="gp")

    dbg_st = {}
    bc_used = 0
    dtin_used = 0
    u_used = [0, 0]
    pd_ev = 0
    sp_used = 0
    delta_used = 0
    prod_used = [0] * NDG
    pop_ev = [max(evb2), max(evb2)]
    m_st = [0, 0]

    for tt in range(NTT):
        ldB = pl.op("sync",
              lambda s, tt=tt: s.dma_start(
                  out=B_NT[:].rearrange("p (n t) -> p n t", n=NST),
                  in_=io["b_scr"][:, tt * TT:(tt + 1) * TT].unsqueeze(0).partition_broadcast(128)),
              waits=[("dve", bc_used)], inc="d_bn", inck=16)
        ldC = pl.op("sync",
              lambda s, tt=tt: s.dma_start(
                  out=C_TN[:].rearrange("p (t n) -> p t n", n=NST),
                  in_=io["ct_scr"][tt * TT:(tt + 1) * TT, :].unsqueeze(0).partition_broadcast(128)),
              waits=[("dve", bc_used)], inc="d_cn", inck=16)
        ldbc = pl.op("sync",
                     lambda s, tt=tt: s.dma_start(out=dt_in[:], in_=io["dt_scr"][:, tt * TT:(tt + 1) * TT]),
                     waits=[("pe", dtin_used)], inc="d_dti", inck=16)

        for dg in range(NDG):
            ub = dg % 2
            pl.op("sync",
                  lambda s, tt=tt, dg=dg, ub=ub: s.dma_start(
                      out=u_in[ub][:], in_=io["u_scr"][dg * 128:(dg + 1) * 128, tt * TT:(tt + 1) * TT]),
                  waits=[("dve", u_used[ub])], inc=f"d_uin{ub}", inck=16)
            ldu = pl.op("sync",
                        lambda s, tt=tt, dg=dg, ub=ub: s.dma_start(
                            out=sres_in[ub][:], in_=io["sres_scr"][dg * 128:(dg + 1) * 128, tt * TT:(tt + 1) * TT]),
                        inc=f"d_uin{ub}", inck=16)

            dtmm = pl.op("tensor",
                         lambda t, dg=dg: t.matmul(p_d[:], dtTl[:, dg * 128:(dg + 1) * 128],
                                                   dt_in[:], start=True, stop=True),
                         waits=[("d_dti", ldbc), ("act", max(dttr_ev, pd_ev))], inc="pe")
            dtin_used = dtmm

            def spf(a, dg=dg):
                a.activation(sp1[:], p_d[:], AF.Abs, bias=dtb_c[:, dg:dg + 1])
                a.activation(sp1[:], sp1[:], AF.Exp, scale=-1.0)
                a.activation(sp1[:], sp1[:], AF.Ln, bias=1.0)
                return a.activation(sp2[:], p_d[:], AF.Relu, bias=dtb_c[:, dg:dg + 1])
            sp = pl.op("scalar", spf, waits=[("pe", dtmm), ("dve", sp_used)], inc="act")
            pd_ev = sp

            def mkdelta(v, ub=ub):
                v.tensor_add(delta_t[:], sp1[:], sp2[:])
                return v.tensor_mul(du_t[:], delta_t[:], u_in[ub][:])
            du = pl.op("vector", mkdelta,
                       waits=[("act", max(sp, delta_used)), (f"d_uin{ub}", ldu),
                              ("d_dbgd", dbg_st.get(("dl",), 0))], inc="dve")
            sp_used = du

            h3 = h_buf[:].rearrange("p (t n) -> p t n", n=NST)
            a_free = 0
            for n in range(NST):
                aex = pl.op("scalar",
                            lambda a, dg=dg, n=n: a.activation(a_t[:], delta_t[:], AF.Exp,
                                                               scale=A_s[dg][:, n:n + 1]),
                            waits=[("dve", max(du, aneg, a_free))], inc="act")
                delta_used = aex

                def scf(v, n=n, dg=dg):
                    v.tensor_mul(dBu_t[:], du_t[:], B_NT[:, n * TT:(n + 1) * TT])
                    return v.tensor_tensor_scan(
                        h3[:, :, n], a_t[:], dBu_t[:],
                        h_state[:, dg * NST + n:dg * NST + n + 1], ALU.mult, ALU.add)
                sc = pl.op("vector", scf,
                           waits=[("act", aex), ("gp", hs0), ("d_bn", ldB)], inc="dve")
                a_free = sc

            def rdf(v, dg=dg, ub=ub):
                v.tensor_mul(hc_buf[:], h_buf[:], C_TN[:])
                v.tensor_reduce(y_t[:].unsqueeze(2),
                                hc_buf[:].rearrange("p (t n) -> p t n", n=NST),
                                AX.X, ALU.add)
                v.tensor_copy(h_state[:, dg * NST:(dg + 1) * NST], h_buf[:, (TT - 1) * NST:TT * NST])
                v.scalar_tensor_tensor(y2_t[:], u_in[ub][:], Dp_c[:, dg:dg + 1], y_t[:],
                                       ALU.mult, ALU.add)
                return v.tensor_mul(prod_all[dg][:], y2_t[:], sres_in[ub][:])
            rd = pl.op("vector", rdf, waits=[("pe", prod_used[dg]), ("d_cn", ldC),
                                             ("d_dbgy", dbg_st.get(("y",), 0))], inc="dve")
            u_used[ub] = rd
            bc_used = rd
            if "y_scr" in io:
                dbg_st[("y",)] = pl.op("sync",
                    lambda s, tt=tt, dg=dg: s.dma_start(
                        out=io["y_scr"][dg * 128:(dg + 1) * 128, tt * TT:(tt + 1) * TT], in_=y2_t[:]),
                    waits=[("dve", rd)], inc="d_dbgy", inck=16)
            if "delta_scr" in io:
                dbg_st[("dl",)] = pl.op("sync",
                    lambda s, tt=tt, dg=dg: s.dma_start(
                        out=io["delta_scr"][dg * 128:(dg + 1) * 128, tt * TT:(tt + 1) * TT], in_=delta_t[:]),
                    waits=[("dve", du)], inc="d_dbgd", inck=16)

        prod_rdy = p.v["dve"]
        for mc in range(8):
            pb = mc % 2
            mb = mc % 2

            def opf(t, mc=mc, pb=pb):
                for dg in range(NDG):
                    i = t.matmul(p_op[pb][:], opT[dg][:, mc * 128:(mc + 1) * 128],
                                 prod_all[dg][:], start=(dg == 0), stop=(dg == NDG - 1))
                return i
            opmm = pl.op("tensor", opf, waits=[("dve", prod_rdy), ("act", pop_ev[pb])], inc="pe")
            for dg in range(NDG):
                prod_used[dg] = opmm

            opev = pl.op("scalar",
                         lambda a, pb=pb, mb=mb: a.activation(m_t[mb][:], p_op[pb][:], AF.Copy),
                         waits=[("pe", opmm), (f"d_mst{mb}", m_st[mb])], inc="act")
            pop_ev[pb] = opev

            m_st[mb] = pl.op("sync",
                             lambda s, tt=tt, mc=mc, mb=mb: s.dma_start(
                                 out=io["m_scr"][mc * 128:(mc + 1) * 128, tt * TT:(tt + 1) * TT],
                                 in_=m_t[mb][:]),
                             waits=[("act", opev)], inc=f"d_mst{mb}", inck=16)


class LNT:

    def __init__(self, p, pl, sb, ps, tag, nslots=4, ptr=None):
        self.p, self.pl, self.tag = p, pl, tag
        self.nslots = nslots
        self.xt = [sb(f"{tag}_xt{i}", [128, D], F32) for i in range(nslots)]
        self.sq = sb(f"{tag}_sq", [128, D], F32)
        self.stat = sb(f"{tag}_stat", [128, 8], F32)
        self.xn_bf = [sb(f"{tag}_xnbf{i}", [128, D], BF16) for i in range(2)]
        self.p_tr = ptr if ptr is not None else [ps(f"{tag}_ptr{i}", [128, 128], BF16)
                                                 for i in range(2)]
        self.eps_c = sb(f"{tag}_eps", [128, 1], F32)
        self.epsv = pl.op("gpsimd", lambda g: g.memset(self.eps_c[:], 1e-5), inc="gp")
        self.ln_prev = {"a3": 0, "g5": 0, "a6": 0, "d7": 0, "d9": 0}
        self.xt_done = [0] * nslots
        self.xnbf_free = [0, 0]
        self.ptr_ev = [0, 0]
        self.slot_rr = 0
        self.pcount = 0

    def emit(self, x_rows_ap, fm_tiles, fm_col, ident_bf, idv, tm_out=None):
        p, pl = self.p, self.pl
        buf = self.slot_rr
        self.slot_rr = (self.slot_rr + 1) % self.nslots
        nb = self.pcount % 2
        self.pcount += 1
        stat, sq, xt, xn_bf, p_tr = self.stat, self.sq, self.xt, self.xn_bf, self.p_tr
        lp = self.ln_prev
        ldx = pl.op("sync",
                    lambda s: s.dma_start(out=xt[buf][:], in_=x_rows_ap),
                    waits=[("dve", self.xt_done[buf])], inc=f"d_{self.tag}x{buf}", inck=16)
        d1 = pl.op("vector", lambda v: v.reduce_sum(stat[:, 0:1], xt[buf][:], AX.X),
                   waits=[(f"d_{self.tag}x{buf}", ldx), ("act", lp["a3"])], inc="dve")
        a2 = pl.op("scalar",
                   lambda a: a.activation(sq[:], xt[buf][:], AF.Square, accum_out=stat[:, 1:2]),
                   waits=[(f"d_{self.tag}x{buf}", ldx), ("gp", lp["g5"])], inc="act")
        a3 = pl.op("scalar",
                   lambda a: a.activation(stat[:, 2:3], stat[:, 0:1], AF.Copy, scale=1.0 / D),
                   waits=[("dve", max(d1, lp["d9"]))], inc="act")
        d4 = pl.op("vector", lambda v: v.tensor_mul(stat[:, 4:5], stat[:, 2:3], stat[:, 2:3]),
                   waits=[("act", a3), ("gp", lp["g5"])], inc="dve")

        def g5f(g):
            g.tensor_scalar_mul(stat[:, 5:6], stat[:, 1:2], 1.0 / D)
            return g.tensor_sub(stat[:, 5:6], stat[:, 5:6], stat[:, 4:5])
        g5 = pl.op("gpsimd", g5f, waits=[("act", max(a2, lp["a6"])), ("dve", d4)], inc="gp")
        a6 = pl.op("scalar",
                   lambda a: a.activation(stat[:, 6:7], stat[:, 5:6], AF.Ln, bias=self.eps_c[:]),
                   waits=[("gp", max(g5, self.epsv)), ("dve", lp["d7"])], inc="act")
        d7 = pl.op("vector", lambda v: v.tensor_copy(stat[:, 3:4], stat[:, 6:7]),
                   waits=[("act", a6)], inc="dve")
        a8 = pl.op("scalar",
                   lambda a: a.activation(stat[:, 7:8], stat[:, 3:4], AF.Exp, scale=-0.5),
                   waits=[("dve", d7)], inc="act")
        d9 = pl.op("vector",
                   lambda v: v.tensor_scalar(xn_bf[nb][:], xt[buf][:], stat[:, 2:3], stat[:, 7:8],
                                             ALU.subtract, ALU.mult),
                   waits=[("act", a8), ("pe", self.xnbf_free[nb])], inc="dve")
        self.ln_prev = {"a3": a3, "g5": g5, "a6": a6, "d7": d7, "d9": d9}
        self.xt_done[buf] = d9
        last = d9
        if tm_out is not None:
            last = pl.op("vector", lambda v: v.tensor_copy(tm_out, xn_bf[nb][:]),
                         waits=[], inc="dve")
        for kc in range(NXC):
            pb = kc % len(p_tr)
            tr = pl.op("tensor",
                       lambda t, kc=kc, pb=pb: t.transpose(
                           p_tr[pb][:], xn_bf[nb][:, kc * 128:(kc + 1) * 128], ident_bf[:]),
                       waits=[("dve", max(last, self.ptr_ev[pb])), ("gp", idv)], inc="pe")
            self.xnbf_free[nb] = tr
            ev = pl.op("vector",
                       lambda v, kc=kc, pb=pb: v.tensor_copy(
                           fm_tiles[kc][:, fm_col:fm_col + 128], p_tr[pb][:]),
                       waits=[("pe", tr)], inc="dve")
            self.ptr_ev[pb] = ev
        return p.v["dve"]


def load_cast_lhsT_natural(pl, wld, dst_bf, w_rows_ap, sem, gate_waits):
    ld = pl.op("sync", lambda s: s.dma_start(out=wld[:], in_=w_rows_ap),
               waits=gate_waits, inc=sem, inck=16)
    cast = pl.op("gpsimd", lambda g: g.tensor_copy(dst_bf, wld[:]),
                 waits=[(sem, ld)], inc="gp")
    return ld, cast


def emit_a2(p, pl, sb, ps, io):
    TT2 = 256
    NT2 = T // TT2
    ident_f32 = sb("a2idf", [128, 128], F32)
    ident_bf = sb("a2idb", [128, 128], BF16)
    idv = make_ident(pl, ident_f32, ident_bf)

    w1T = [sb(f"a2w1T{kc}", [128, 4096], BF16) for kc in range(NXC)]
    w2T = [sb(f"a2w2T{hc}", [128, D], BF16) for hc in range(32)]
    pst = [ps(f"a2pst{i}", [128, 128], F32) for i in range(2)]
    wldsh = [sb(f"a2wld{i}", [128, 4096], F32) for i in range(2)]
    evb = [0, 0]
    trs = [0, 0]
    prep_lhsT(pl, sb, ps, io["ffn_w1"], 4096, D, w1T, ident_f32, "a2w1", pst, evb,
              wld=wldsh, tr_state=trs)
    prep_lhsT(pl, sb, ps, io["ffn_w2"], D, 4096, w2T, ident_f32, "a2w2", pst, evb,
              wld=wldsh, tr_state=trs)

    b1c = sb("a2b1c", [128, 32], F32)
    b2c = sb("a2b2c", [128, 8], F32)
    for hc in range(32):
        col_load(pl, b1c[:, hc:hc + 1], io["ffn_b1"][hc * 128:(hc + 1) * 128], sem="d_a2c")
    for oc in range(8):
        ldc = col_load(pl, b2c[:, oc:oc + 1], io["ffn_b2"][oc * 128:(oc + 1) * 128], sem="d_a2c")

    m_in = [sb(f"a2m{kc}", [128, TT2], F32) for kc in range(NXC)]
    m_bf = [sb(f"a2mb{kc}", [128, TT2], BF16) for kc in range(NXC)]
    h_bf = [sb(f"a2h{hc}", [128, TT2], BF16) for hc in range(32)]
    f_t = sb("a2f", [128, TT2], F32)
    c_t = sb("a2c", [128, TT2], F32)
    csq = sb("a2csq", [128, TT2], F32)
    ctm = [sb(f"a2ctm{i}", [128, D], F32) for i in range(2)]
    stats1 = sb("a2s1", [128, NT2 * 8], F32)
    stats2 = sb("a2s2", [128, NT2 * 8], F32)
    sfin = sb("a2sf", [1, 2], F32)
    p_h = [ps(f"a2ph{i}", [128, TT2], F32) for i in range(2)]
    p_f = [ps(f"a2pf{i}", [128, TT2], F32) for i in range(2)]
    p_ctr = [ps(f"a2ptr{i}", [128, 128], F32) for i in range(2)]
    p_ssum = pst[0][0:1, 0:2]

    m_done = [0] * NXC
    mbf_done = [0] * NXC
    ph_ev = [0, 0]
    pf_ev = [0, 0]
    hbf_used = [0] * 32
    ct_free = [0, 0]
    ctm_st = [0, 0]
    ctr_ev = [0, 0]
    c_used = 0

    for tt in range(NT2):
        for kc in range(NXC):
            ldm = pl.op("sync",
                        lambda s, tt=tt, kc=kc: s.dma_start(
                            out=m_in[kc][:], in_=io["m_scr"][kc * 128:(kc + 1) * 128, tt * TT2:(tt + 1) * TT2]),
                        waits=[("dve", m_done[kc])], inc=f"d_a2m{kc}", inck=16)
            cast = pl.op("gpsimd",
                         lambda g, kc=kc: g.tensor_copy(m_bf[kc][:], m_in[kc][:]),
                         waits=[(f"d_a2m{kc}", ldm), ("pe", mbf_done[kc])], inc="gp")
            m_done[kc] = p.v["dve"]
            mbf_done[kc] = cast
        cast_all = p.v["gp"]

        for hc in range(32):
            pb = hc % 2

            def hmm(t, hc=hc, pb=pb, rdy=cast_all):
                for kc in range(NXC):
                    i = t.matmul(p_h[pb][:], w1T[kc][:, hc * 128:(hc + 1) * 128],
                                 m_bf[kc][:], start=(kc == 0), stop=(kc == NXC - 1))
                return i
            mm = pl.op("tensor", hmm, waits=[("gp", cast_all), ("act", max(ph_ev[pb], max(evb))),
                                             ("pe", hbf_used[hc])], inc="pe")
            ev = pl.op("scalar",
                       lambda a, hc=hc, pb=pb: a.activation(h_bf[hc][:], p_h[pb][:], AF.Gelu,
                                                            bias=b1c[:, hc:hc + 1]),
                       waits=[("pe", mm), ("d_a2c", ldc)], inc="act")
            ph_ev[pb] = ev
        h_ready = p.v["act"]
        for kc in range(NXC):
            mbf_done[kc] = p.v["pe"]

        for oc in range(8):
            pb = oc % 2

            def fmm(t, oc=oc, pb=pb, rdy=h_ready):
                for hc in range(32):
                    i = t.matmul(p_f[pb][:], w2T[hc][:, oc * 128:(oc + 1) * 128],
                                 h_bf[hc][:], start=(hc == 0), stop=(hc == 31))
                return i
            mm = pl.op("tensor", fmm, waits=[("act", h_ready), ("dve", pf_ev[pb])], inc="pe")
            for hc in range(32):
                hbf_used[hc] = mm

            fe = pl.op("scalar",
                       lambda a, oc=oc, pb=pb: a.activation(f_t[:], p_f[pb][:], AF.Identity,
                                                            bias=b2c[:, oc:oc + 1]),
                       waits=[("pe", mm), ("dve", c_used)], inc="act")

            cc = pl.op("vector",
                       lambda v, oc=oc, pb=pb: v.tensor_add(c_t[:], f_t[:], m_in[oc][:]),
                       waits=[("act", fe)], inc="dve")
            pf_ev[pb] = cc
            m_done[oc] = cc
            c_used = cc

            st1 = pl.op("vector",
                        lambda v, tt=tt, oc=oc: v.reduce_sum(stats1[:, tt * 8 + oc:tt * 8 + oc + 1],
                                                             c_t[:], AX.X),
                        inc="dve")
            st2 = pl.op("scalar",
                        lambda a, tt=tt, oc=oc: a.activation(csq[:], c_t[:], AF.Square,
                                                             accum_out=stats2[:, tt * 8 + oc:tt * 8 + oc + 1]),
                        waits=[("dve", cc)], inc="act")
            c_used = max(c_used, p.v["dve"])

            for pi2 in range(2):
                pb2 = pi2 % 2
                tr = pl.op("tensor",
                           lambda t, pi2=pi2, pb2=pb2: t.transpose(
                               p_ctr[pb2][:], c_t[:, pi2 * 128:(pi2 + 1) * 128], ident_f32[:]),
                           waits=[("dve", max(st1, ctr_ev[pb2])), ("act", st2), ("gp", idv)], inc="pe")
                c_used = max(c_used, p.v["dve"])

                def trev(v, oc=oc, pi2=pi2, pb2=pb2):
                    return v.tensor_copy(ctm[pi2][:, oc * 128:(oc + 1) * 128], p_ctr[pb2][:])
                ev2 = pl.op("vector", trev,
                            waits=[("pe", tr), (f"d_a2ct{pi2}", ctm_st[pi2])], inc="dve")
                ctr_ev[pb2] = ev2
            c_used = p.v["dve"]

        for pi2 in range(2):
            ctm_st[pi2] = pl.op("sync",
                                lambda s, tt=tt, pi2=pi2: s.dma_start(
                                    out=io["c_scr"][tt * TT2 + pi2 * 128:tt * TT2 + (pi2 + 1) * 128, :],
                                    in_=ctm[pi2][:]),
                                waits=[("dve", p.v["dve"])], inc=f"d_a2ct{pi2}", inck=16)

    ones_c = sb("a2ones", [128, 1], F32)
    colsum = sb("a2colsum", [128, 2], F32)
    pl.op("gpsimd", lambda g: g.memset(ones_c[:], 1.0), inc="gp")
    ones_v = p.v["gp"]

    def redf(v):
        v.reduce_sum(colsum[:, 0:1], stats1[:], AX.X)
        return v.reduce_sum(colsum[:, 1:2], stats2[:], AX.X)
    redv = pl.op("vector", redf, waits=[("dve", p.v["dve"]), ("act", p.v["act"])], inc="dve")
    smm = pl.op("tensor",
                lambda t: t.matmul(p_ssum, ones_c[:], colsum[:], start=True, stop=True),
                waits=[("gp", ones_v), ("dve", redv), ("act", max(evb))], inc="pe")
    sev = pl.op("vector", lambda v: v.tensor_copy(sfin[:], p_ssum),
                waits=[("pe", smm)], inc="dve")
    pl.op("sync", lambda s: s.dma_start(out=io["stats_scr"], in_=sfin[:]),
          waits=[("dve", sev)], inc="d_a2st", inck=16)


def emit_a3(p, pl, sb, ps, io):
    NTOK = 512
    ident_f32 = sb("a3idf", [128, 128], F32)
    ident_bf = sb("a3idb", [128, 128], BF16)
    idv = make_ident(pl, ident_f32, ident_bf)

    xn_fm = [sb(f"a3xn{kc}", [128, NTOK], BF16) for kc in range(NXC)]
    lnt = LNT(p, pl, sb, ps, "a3ln")
    for pi in range(4):
        lnt.emit(io["x0s"][pi * 128:(pi + 1) * 128, :], xn_fm, pi * 128, ident_bf, idv)
    xn_ready = p.v["dve"]

    ew_t = sb("a3ew", [128, 8], F32)
    ws_c = sb("a3ws", [128, 4], F32)
    ws_tr = sb("a3wstr", [4, 128], F32)
    w_bc = sb("a3wbc", [128, NTOK], F32)

    ldew = pl.op("sync", lambda s: s.dma_start(
        out=ew_t[:].rearrange("p (q e) -> p q e", e=2),
        in_=io["ews"].rearrange("(q p) e -> p q e", p=128)), inc="d_a3ew", inck=16)
    wsv = pl.op("vector",
                lambda v: v.tensor_reduce(ws_c[:].unsqueeze(2),
                                          ew_t[:].rearrange("p (q e) -> p q e", e=2),
                                          AX.X, ALU.add),
                waits=[("d_a3ew", ldew)], inc="dve")
    wtr = pl.op("tensor", lambda t: t.transpose(p_mtr[0][0:4, :], ws_c[:], ident_f32[:]),
                waits=[("dve", wsv), ("gp", idv)], inc="pe")
    wev = pl.op("vector", lambda v: v.tensor_copy(ws_tr[:], p_mtr[0][0:4, :]),
                waits=[("pe", wtr)], inc="dve")
    wst = pl.op("sync", lambda s: s.dma_start(out=io["ws_scr"], in_=ws_tr[:]),
                waits=[("dve", wev)], inc="d_a3ws", inck=16)
    ldbc = pl.op("sync", lambda s: s.dma_start(
        out=w_bc[:], in_=io["ws_scr"].rearrange("q p -> (q p)").unsqueeze(0).partition_broadcast(128)),
        waits=[("d_a3ws", wst)], inc="d_a3wb", inck=16)

    wld = [sb(f"a3wld{i}", [128, 2048], F32) for i in range(2)]
    w1q = [sb(f"a3w1q{kc}", [128, 2048], BF16) for kc in range(NXC)]
    w2q = [sb(f"a3w2q{hc}", [128, D], BF16) for hc in range(16)]
    g_bf = [sb(f"a3g{hc}", [128, NTOK], BF16) for hc in range(16)]
    y_acc = [sb(f"a3y{oc}", [128, NTOK], F32) for oc in range(8)]
    mo_t = sb("a3mo", [128, NTOK], F32)
    tm_buf = [sb(f"a3tm{pi}", [128, D], F32) for pi in range(4)]
    p_h = [ps(f"a3ph{i}", [128, NTOK], F32) for i in range(2)]
    p_y = [ps(f"a3py{i}", [128, NTOK], F32) for i in range(2)]
    p_mtr = [ps(f"a3ptr{i}", [128, 128], F32) for i in range(2)]

    wld_used = [0, 0]
    w1q_used = [0] * NXC
    w2q_used = [0] * 16
    ph_ev = [0, 0]
    py_ev = [0, 0]
    g_used = [0] * 16
    yacc_dve = [0] * 8

    for q in range(4):
        w1c = [0] * NXC
        for kc in range(NXC):
            buf = kc % 2
            ld = pl.op("sync",
                       lambda s, q=q, kc=kc, buf=buf: s.dma_start(
                           out=wld[buf][:], in_=io["moe_w1"][kc * 128:(kc + 1) * 128,
                                                             q * 2048:(q + 1) * 2048]),
                       waits=[("gp", wld_used[buf])], inc=f"d_a3w{buf}", inck=16)
            cast = pl.op("gpsimd",
                         lambda g, kc=kc, buf=buf: g.tensor_copy(w1q[kc][:], wld[buf][:]),
                         waits=[(f"d_a3w{buf}", ld), ("pe", w1q_used[kc])], inc="gp")
            wld_used[buf] = cast
            w1c[kc] = cast
        w2c = [0] * 16
        for hc in range(16):
            buf = hc % 2
            ld = pl.op("sync",
                       lambda s, q=q, hc=hc, buf=buf: s.dma_start(
                           out=wld[buf][:, 0:D], in_=io["moe_w2"][q * 2048 + hc * 128:
                                                                  q * 2048 + (hc + 1) * 128, :]),
                       waits=[("gp", wld_used[buf])], inc=f"d_a3w{buf}", inck=16)
            cast = pl.op("gpsimd",
                         lambda g, hc=hc, buf=buf: g.tensor_copy(w2q[hc][:], wld[buf][:, 0:D]),
                         waits=[(f"d_a3w{buf}", ld), ("pe", w2q_used[hc])], inc="gp")
            wld_used[buf] = cast
            w2q_used[hc] = cast
            w2c[hc] = cast

        for hc in range(16):
            pb = hc % 2

            def hmm(t, hc=hc, pb=pb, w1c=tuple(w1c)):
                for kc in range(NXC):
                    i = t.matmul(p_h[pb][:], w1q[kc][:, hc * 128:(hc + 1) * 128],
                                 xn_fm[kc][:], start=(kc == 0), stop=(kc == NXC - 1))
                return i
            mm = pl.op("tensor", hmm,
                       waits=[("gp", max(w1c)), ("dve", xn_ready), ("act", ph_ev[pb]),
                              ("pe", g_used[hc])], inc="pe")
            w1q_used_v = mm
            for kc in range(NXC):
                w1q_used[kc] = mm
            ev = pl.op("scalar",
                       lambda a, hc=hc, pb=pb: a.activation(g_bf[hc][:], p_h[pb][:],
                                                            AF.Gelu_apprx_tanh),
                       waits=[("pe", mm)], inc="act")
            ph_ev[pb] = ev
        g_ready = p.v["act"]

        for oc in range(8):
            pb = oc % 2

            def ymm(t, oc=oc, pb=pb):
                for hc in range(16):
                    i = t.matmul(p_y[pb][:], w2q[hc][:, oc * 128:(oc + 1) * 128],
                                 g_bf[hc][:], start=(hc == 0), stop=(hc == 15))
                return i
            mm = pl.op("tensor", ymm, waits=[("act", g_ready), ("dve", py_ev[pb]),
                                             ("gp", max(w2c))], inc="pe")
            for hc in range(16):
                w2q_used[hc] = mm
                g_used[hc] = mm

            if q == 0:
                ye = pl.op("vector",
                           lambda v, oc=oc, pb=pb: v.tensor_copy(y_acc[oc][:], p_y[pb][:]),
                           waits=[("pe", mm)], inc="dve")
            else:
                ye = pl.op("vector",
                           lambda v, oc=oc, pb=pb: v.tensor_add(y_acc[oc][:], y_acc[oc][:],
                                                                p_y[pb][:]),
                           waits=[("pe", mm)], inc="dve")
            py_ev[pb] = ye
            yacc_dve[oc] = ye

    mo_used = 0
    mtr_bank = [wev, 0]
    mo_st = [0] * 4
    for oc in range(8):
        sc = pl.op("vector",
                   lambda v, oc=oc: v.tensor_mul(mo_t[:], y_acc[oc][:], w_bc[:]),
                   waits=[("dve", yacc_dve[oc]), ("d_a3wb", ldbc), ("pe", mo_used)], inc="dve")
        for pi in range(4):
            pb = pi % 2
            tr = pl.op("tensor",
                       lambda t, pi=pi, pb=pb: t.transpose(p_mtr[pb][:],
                                                           mo_t[:, pi * 128:(pi + 1) * 128],
                                                           ident_f32[:]),
                       waits=[("dve", max(sc, mtr_bank[pb]))], inc="pe")
            mo_used = tr
            ev = pl.op("vector",
                       lambda v, oc=oc, pi=pi, pb=pb: v.tensor_copy(
                           tm_buf[pi][:, oc * 128:(oc + 1) * 128], p_mtr[pb][:]),
                       waits=[("pe", tr), (f"d_a3mo{pi}", mo_st[pi])], inc="dve")
            mtr_bank[pb] = ev
    for pi in range(4):
        mo_st[pi] = pl.op("sync",
                          lambda s, pi=pi: s.dma_start(
                              out=io["moe_out"][pi * 128:(pi + 1) * 128, :], in_=tm_buf[pi][:]),
                          waits=[("dve", p.v["dve"])], inc=f"d_a3mo{pi}", inck=16)


def emit_b(p, pl, sb, ps, io):
    st_in = sb("b_stin", [1, 2], F32)
    st_mr = sb("b_stmr", [1, 2], F32)
    mrb = sb("b_mrb", [128, 2], F32)
    msk_c = sb("b_msk", [128, 1], F32)
    eps_c = sb("b_eps", [128, 1], F32)
    epsv = pl.op("gpsimd", lambda g: g.memset(eps_c[0:1, :], 1e-5), inc="gp")
    lds = pl.op("sync", lambda s: s.dma_start(out=st_in[:], in_=io["stats_scr"]),
                inc="d_bst", inck=16)
    ldm = pl.op("sync", lambda s: s.dma_start(
        out=msk_c[:], in_=io["msk"].unsqueeze(0).partition_broadcast(128)),
        inc="d_bmk", inck=16)
    a1 = pl.op("scalar",
               lambda a: a.activation(st_mr[:, 0:1], st_in[:, 0:1], AF.Copy, scale=1.0 / (T * D)),
               waits=[("d_bst", lds)], inc="act")
    g1 = pl.op("gpsimd",
               lambda g: g.tensor_scalar_mul(st_in[:, 1:2], st_in[:, 1:2], 1.0 / (T * D)),
               waits=[("d_bst", lds), ("act", a1)], inc="gp")
    d1 = pl.op("vector", lambda v: v.tensor_mul(st_mr[:, 1:2], st_mr[:, 0:1], st_mr[:, 0:1]),
               waits=[("act", a1)], inc="dve")

    def g2f(g):
        return g.tensor_sub(st_in[:, 1:2], st_in[:, 1:2], st_mr[:, 1:2])
    g2 = pl.op("gpsimd", g2f, waits=[("gp", g1), ("dve", d1)], inc="gp")
    a2 = pl.op("scalar",
               lambda a: a.activation(st_in[:, 1:2], st_in[:, 1:2], AF.Ln, bias=eps_c[0:1, :]),
               waits=[("gp", max(g2, epsv))], inc="act")
    d2 = pl.op("vector", lambda v: v.tensor_copy(st_mr[:, 1:2], st_in[:, 1:2]),
               waits=[("act", a2)], inc="dve")
    a3 = pl.op("scalar",
               lambda a: a.activation(st_mr[:, 1:2], st_mr[:, 1:2], AF.Exp, scale=-0.5),
               waits=[("dve", d2)], inc="act")
    stw = pl.op("sync", lambda s: s.dma_start(out=io["mr_scr"], in_=st_mr[:]),
                waits=[("act", a3)], inc="d_bmr", inck=16)
    ldmr = pl.op("sync", lambda s: s.dma_start(
        out=mrb[:], in_=io["mr_scr"].partition_broadcast(128)),
        waits=[("d_bmr", stw)], inc="d_bmrb", inck=16)

    xi = [sb(f"b_x{i}", [128, D], F32) for i in range(2)]
    ci = [sb(f"b_c{i}", [128, D], F32) for i in range(2)]
    mi = [sb(f"b_m{i}", [128, D], F32) for i in range(2)]
    o1 = [sb(f"b_o{i}", [128, D], F32) for i in range(2)]
    used = [0, 0]
    ost = [0, 0]
    for ti in range(T // 128):
        bu = ti % 2

        pl.op("sync", lambda s, ti=ti, bu=bu: s.dma_start(
            out=xi[bu][:], in_=io["x"][ti * 128:(ti + 1) * 128, :]),
            waits=[("dve", used[bu])], inc=f"d_bx{bu}", inck=16)
        pl.op("sync", lambda s, ti=ti, bu=bu: s.dma_start(
            out=ci[bu][:], in_=io["c_scr"][ti * 128:(ti + 1) * 128, :]),
            inc=f"d_bx{bu}", inck=16)
        ld = pl.op("sync", lambda s, ti=ti, bu=bu: s.dma_start(
            out=mi[bu][:], in_=io["moe_full"][ti * 128:(ti + 1) * 128, :]),
            inc=f"d_bx{bu}", inck=16)

        def comb(v, bu=bu):
            v.tensor_scalar(ci[bu][:], ci[bu][:], mrb[:, 0:1], mrb[:, 1:2],
                            ALU.subtract, ALU.mult)
            v.tensor_add(xi[bu][:], xi[bu][:], ci[bu][:])
            return v.scalar_tensor_tensor(o1[bu][:], mi[bu][:], msk_c[:], xi[bu][:],
                                          ALU.mult, ALU.add)
        cb = pl.op("vector", comb,
                   waits=[(f"d_bx{bu}", ld), ("d_bmrb", ldmr), ("d_bmk", ldm),
                          (f"d_bo{bu}", ost[bu])], inc="dve")
        used[bu] = cb
        ost[bu] = pl.op("sync",
                        lambda s, ti=ti, bu=bu: s.dma_start(
                            out=io["x1_scr"][ti * 128:(ti + 1) * 128, :], in_=o1[bu][:]),
                        waits=[("dve", cb)], inc=f"d_bo{bu}", inck=16)


def declare_l0_inputs(nc, io):
    io["in_proj_w"] = nc.dram_tensor("in_proj_w", [2 * DIN, D], F32, kind="ExternalInput").ap()
    io["conv_w"] = nc.dram_tensor("conv_w", [DIN, 1, 4], F32, kind="ExternalInput").ap()
    io["conv_b"] = nc.dram_tensor("conv_b", [DIN], F32, kind="ExternalInput").ap()
    io["x_proj_w"] = nc.dram_tensor("x_proj_w", [96, DIN], F32, kind="ExternalInput").ap()
    io["dt_proj_w"] = nc.dram_tensor("dt_proj_w", [DIN, DTR], F32, kind="ExternalInput").ap()
    io["dt_proj_b"] = nc.dram_tensor("dt_proj_b", [DIN], F32, kind="ExternalInput").ap()
    io["A_log"] = nc.dram_tensor("A_log", [DIN, NST], F32, kind="ExternalInput").ap()
    io["Dparam"] = nc.dram_tensor("Dparam", [DIN], F32, kind="ExternalInput").ap()
    io["out_proj_w"] = nc.dram_tensor("out_proj_w", [D, DIN], F32, kind="ExternalInput").ap()


def build_a1_test(debug=True):
    nc = bass.Bass("TRN2", target_bir_lowering=False)
    io = {}
    io["x"] = nc.dram_tensor("x", [T, D], F32, kind="ExternalInput").ap()
    declare_l0_inputs(nc, io)
    kind = "ExternalOutput" if debug else "Internal"
    for name, shape, dt in [
        ("u_scr", [DIN, T], F32), ("sres_scr", [DIN, T], BF16),
        ("dt_scr", [DTR, T], BF16), ("b_scr", [NST, T], BF16),
        ("ct_scr", [T, NST], BF16),
    ]:
        io[name] = nc.dram_tensor(name, shape, dt, kind=kind).ap()
    io["m_scr"] = nc.dram_tensor("m_scr", [D, T], F32, kind="ExternalOutput").ap()
    if debug:
        io["y_scr"] = nc.dram_tensor("y_scr", [DIN, T], F32, kind="ExternalOutput").ap()
        io["delta_scr"] = nc.dram_tensor("delta_scr", [DIN, T], F32, kind="ExternalOutput").ap()

    plans = []
    with ExitStack() as ctx:
        p = Prog(nc, ctx)
        with ExitStack() as c1:
            sb = lambda n, s, dt=F32: c1.enter_context(nc.sbuf_tensor(n, s, dt))
            ps_ = lambda n, s, dt=F32: c1.enter_context(nc.psum_tensor(n, s, dt))
            pl = Plan(p)
            emit_a1a(p, pl, sb, ps_, io)
            pl.barrier()
            plans.append(pl)
            pl.realize(nc)
        with ExitStack() as c2:
            sb = lambda n, s, dt=F32: c2.enter_context(nc.sbuf_tensor(n, s, dt))
            ps_ = lambda n, s, dt=F32: c2.enter_context(nc.psum_tensor(n, s, dt))
            pl = Plan(p)
            emit_a1b(p, pl, sb, ps_, io)
            pl.barrier()
            plans.append(pl)
            pl.realize(nc)
    check_plans(plans)
    return nc


HPP = 4
NPASS = 4
DH = 64
NHEAD = 16


def emit_c_pass(p, pl, sb, ps, io, hp):
    sb0, ps0 = sb, ps
    sb = lambda n, s, dt=F32: sb0(f"h{hp}{n}", s, dt)
    ps = lambda n, s, dt=F32: ps0(f"h{hp}{n}", s, dt)
    ident_f32 = sb("cidf", [128, 128], F32)
    ident_bf = sb("cidb", [128, 128], BF16)
    idv = make_ident(pl, ident_f32, ident_bf)

    wstage = sb("c_wst", [128, D], F32)
    qw = {w: [sb(f"c_{w}w{kc}", [128, HPP * DH], BF16) for kc in range(NXC)]
          for w in "qkv"}
    stg_used = 0
    for w_i, w in enumerate("qkv"):
        for kc in range(NXC):
            col0 = w_i * D + hp * HPP * DH
            ld = pl.op("sync",
                       lambda s, kc=kc, col0=col0: s.dma_start(
                           out=wstage[:, 0:HPP * DH],
                           in_=io["qkv_w"][kc * 128:(kc + 1) * 128, col0:col0 + HPP * DH]),
                       waits=[("gp", stg_used)], inc="d_cwst", inck=16)
            cast = pl.op("gpsimd",
                         lambda g, w=w, kc=kc: g.tensor_copy(qw[w][kc][:], wstage[:, 0:HPP * DH]),
                         waits=[("d_cwst", ld)], inc="gp")
            stg_used = cast
    qb_c = sb("c_qbc", [128, 6], F32)
    for w_i in range(3):
        for cc2 in range(2):
            col0 = w_i * D + hp * HPP * DH + cc2 * 128
            col_load(pl, qb_c[:, w_i * 2 + cc2:w_i * 2 + cc2 + 1],
                     io["qkv_b"][col0:col0 + 128], sem="d_ccst")
    now = 8 if hp == 0 else 2
    outw = [sb(f"c_ow{r}", [128, D], BF16) for r in range(now)]
    for r in range(now):
        row0 = r * 128 if hp == 0 else hp * HPP * DH + r * 128
        ld = pl.op("sync",
                   lambda s, row0=row0: s.dma_start(
                       out=wstage[:, 0:D], in_=io["out_w"][row0:row0 + 128, :]),
                   waits=[("gp", stg_used)], inc="d_cwst", inck=16)
        cast = pl.op("gpsimd", lambda g, r=r: g.tensor_copy(outw[r][:], wstage[:, 0:D]),
                     waits=[("d_cwst", ld)], inc="gp")
        stg_used = cast
    ob_c = sb("c_obc", [128, 8], F32)
    if hp == 0:
        for oc in range(8):
            col_load(pl, ob_c[:, oc:oc + 1], io["out_b"][oc * 128:(oc + 1) * 128], sem="d_ccst")
    qkvw_ready = p.v["gp"]
    cst_all = p.v["d_ccst"]

    Kst = [sb(f"c_K{h}", [65, T], BF16) for h in range(HPP)]
    Vst = [sb(f"c_V{h}", [128, 32 * 65], BF16) for h in range(HPP)]
    Qt = [sb(f"c_Q{h}", [65, TT], BF16) for h in range(HPP)]

    def initones(g):
        for h in range(HPP):
            g.memset(Kst[h][64:65, :], 1.0)
            i = g.memset(Vst[h][:], 1.0)
        return i
    ones_v = pl.op("gpsimd", initones, inc="gp")

    mtok = [sb(f"c_mt{qc}", [128, TT], BF16) for qc in range(4)]
    mkey = [sb(f"c_mk{v}", [128, TT], BF16) for v in range(4)]

    def mkmasks(g):
        for qc in range(4):
            g.memset(mtok[qc][:], 0.0)
            g.affine_select(out=mtok[qc][:], in_=mtok[qc][:], compare_op=ALU.is_ge,
                            fill=-1e30, base=qc * 128, pattern=[[-1, TT]],
                            channel_multiplier=1)
        for v in range(4):
            g.memset(mkey[v][:], 0.0)
            i = g.affine_select(out=mkey[v][:], in_=mkey[v][:], compare_op=ALU.is_ge,
                                fill=-1e30, base=-128 * v, pattern=[[1, TT]],
                                channel_multiplier=-1)
        return i
    masks_v = pl.op("gpsimd", mkmasks, inc="gp")

    p_vtr = ps("c_pvt", [128, 128], BF16)
    xn_fm = [sb(f"c_xn{kc}", [128, TT], BF16) for kc in range(NXC)]
    lnt = LNT(p, pl, sb, ps, f"cl{hp}", nslots=2, ptr=[p_vtr])
    p_s = [ps(f"c_ps{i}", [128, TT], F32) for i in range(2)]
    p_o = ps("c_po", [65, TT], F32)
    p_ow = [ps(f"c_pow{i}", [128, TT], F32) for i in range(2)]
    p_qkv = p_ow

    vtmp = sb("c_vtmp", [64, TT], BF16)
    nrm_all = sb("c_nrma", [128, 32], F32)
    nrm2 = sb("c_nrm2", [128, 32], F32)
    nr4f = sb("c_nr4f", [128, 4], F32)
    nr4i = sb("c_nr4i", [128, 4], BF16)
    nr4b = sb("c_nr4b", [4, 128], BF16)
    Pt = [sb(f"c_P{i}", [128, TT], BF16) for i in range(2)]
    rden = sb("c_rden", [1, TT], F32)
    den_bc = sb("c_denb", [64, TT], F32)
    seq_fm = [sb(f"c_sf{i}", [128, TT], BF16) for i in range(HPP // 2)]
    ap_t = [sb(f"c_ap{i}", [128, TT], F32) for i in range(2)]
    ao_t = [sb(f"c_ao{i}", [128, TT], F32) for i in range(2)]

    import os
    nomem = os.environ.get("C_NOMEM") == "1"
    mem = {}
    if hp == 0 and not nomem:
        mem = emit_c_mem_setup(p, pl, sb, ps, io, ident_f32, ident_bf, idv)
        mem["idf"] = ident_f32
        mem["idb"] = ident_bf

    pqkv_ev = [0, 0]
    vtr_ev = 0
    vtmp_used = 0
    ps_act = [0, 0]
    ps_dve = [0, 0]
    pt_used = [0, 0]
    po_ev = 0
    sf_used = [0] * (HPP // 2)
    ap_ld = [0, 0]
    ao_st = [0, 0]
    pow_rd = [0, 0]
    pow_rd_kind = ["act", "act"]
    den_st = 0
    denb_used = 0
    qt_used = 0
    ntr_prev = 0
    nr4i_used = 0
    nrm_used = 0
    nr4f_used = 0

    for qt in range(NTT):
        lnt.ptr_ev[0] = max(lnt.ptr_ev[0], vtr_ev)
        for pi in range(4):
            lnt.emit(io["x1_scr"][(qt * 4 + pi) * 128:(qt * 4 + pi + 1) * 128, :],
                     xn_fm, pi * 128, ident_bf, idv)
        xn_rdy = p.v["dve"]
        vtr_ev = max(vtr_ev, xn_rdy)

        if hp == 0 and not nomem:
            mem["p_m"] = p_s
            mem["ps_act"] = ps_act
            mem["ps_dve"] = ps_dve
            emit_c_mem_qtile(p, pl, sb, ps, io, mem, qt, xn_fm, xn_rdy)
            bkq = mem["bk"]
            ps_act[0] = max(ps_act[0], bkq.get("pm_ev0", 0), bkq.get("gm_ev", 0))
            ps_act[1] = max(ps_act[1], bkq.get("pm_ev1", 0))

        for w_i, w in enumerate("qkv"):
            for cc2 in range(2):
                pb = cc2

                def qmm(t, w=w, cc2=cc2, pb=pb):
                    for kc in range(NXC):
                        i = t.matmul(p_qkv[pb][:], qw[w][kc][:, cc2 * 128:(cc2 + 1) * 128],
                                     xn_fm[kc][:], start=(kc == 0), stop=(kc == NXC - 1))
                    return i
                mm = pl.op("tensor", qmm,
                           waits=[("dve", xn_rdy), ("gp", qkvw_ready),
                                  ("act", pqkv_ev[pb]),
                                  (pow_rd_kind[pb], pow_rd[pb])], inc="pe")
                hA, hB = cc2 * 2, cc2 * 2 + 1
                bcol = qb_c[:, w_i * 2 + cc2:w_i * 2 + cc2 + 1]
                if w == "q":
                    def qev(a, hA=hA, hB=hB, pb=pb, bcol=bcol):
                        a.activation(Qt[hA][0:64, :], p_qkv[pb][0:64, :], AF.Identity,
                                     bias=bcol[0:64, :])
                        return a.activation(Qt[hB][0:64, :], p_qkv[pb][64:128, :], AF.Identity,
                                            bias=bcol[64:128, :])
                    ev = pl.op("scalar", qev,
                               waits=[("pe", max(mm, qt_used)), ("d_ccst", cst_all)], inc="act")
                    pqkv_ev[pb] = ev
                elif w == "k":
                    def kev(a, hA=hA, hB=hB, pb=pb, qt=qt, bcol=bcol):
                        a.activation(Kst[hA][0:64, qt * TT:(qt + 1) * TT], p_qkv[pb][0:64, :],
                                     AF.Identity, bias=bcol[0:64, :])
                        return a.activation(Kst[hB][0:64, qt * TT:(qt + 1) * TT],
                                            p_qkv[pb][64:128, :], AF.Identity,
                                            bias=bcol[64:128, :])
                    ev = pl.op("scalar", kev, waits=[("pe", mm), ("d_ccst", cst_all)], inc="act")
                    pqkv_ev[pb] = ev
                else:
                    for hh, base in ((hA, 0), (hB, 64)):
                        ve = pl.op("scalar",
                                   lambda a, base=base, pb=pb, bcol=bcol:
                                   a.activation(vtmp[:], p_qkv[pb][base:base + 64, :],
                                                AF.Identity, bias=bcol[base:base + 64, :]),
                                   waits=[("pe", max(mm, vtmp_used)), ("d_ccst", cst_all)],
                                   inc="act")
                        pqkv_ev[pb] = ve
                        for q4 in range(4):
                            tr = pl.op("tensor",
                                       lambda t, q4=q4: t.transpose(
                                           p_vtr[0:128, 0:64], vtmp[:, q4 * 128:(q4 + 1) * 128],
                                           ident_bf[0:64, 0:64]),
                                       waits=[("act", ve), ("dve", vtr_ev), ("gp", ones_v)],
                                       inc="pe")
                            vtmp_used = tr
                            j = qt * 4 + q4
                            vev = pl.op("vector",
                                        lambda v, hh=hh, j=j: v.tensor_copy(
                                            Vst[hh][:, j * 65:j * 65 + 64], p_vtr[0:128, 0:64]),
                                        waits=[("pe", tr)], inc="dve")
                            vtr_ev = vev
        kv_ready_act = p.v["act"]

        for h in range(HPP):
            for j5 in range(qt + 1):
                diag = (j5 == qt)
                for qc in range(4):
                    pb = (j5 * 4 + qc) % 2

                    def smm(t, h=h, qc=qc, j5=j5, pb=pb):
                        return t.matmul(p_s[pb][:], Qt[h][0:64, qc * 128:(qc + 1) * 128],
                                        Kst[h][0:64, j5 * TT:(j5 + 1) * TT],
                                        start=True, stop=True)
                    mm = pl.op("tensor", smm,
                               waits=[("act", max(kv_ready_act, ps_act[pb])),
                                      ("dve", ps_dve[pb])], inc="pe")
                    red_w = [("pe", mm), ("gp", nrm_used)]
                    if diag:
                        msk = pl.op("vector",
                                    lambda v, pb=pb, qc=qc: v.tensor_add(
                                        p_s[pb][:], p_s[pb][:], mtok[qc][:]),
                                    waits=[("pe", mm), ("gp", masks_v)], inc="dve")
                        red_w = [("dve", msk), ("gp", nrm_used)]
                    red = pl.op("vector",
                                lambda v, pb=pb, qc=qc, j5=j5: v.reduce_max(
                                    nrm_all[:, qc * 8 + j5:qc * 8 + j5 + 1], p_s[pb][:],
                                    AX.X, negate=True),
                                waits=red_w, inc="dve")
                    ps_dve[pb] = red
            reds_done = p.v["dve"]
            g2 = pl.op("gpsimd", lambda g: g.tensor_copy(nrm2[:], nrm_all[:]),
                       waits=[("dve", reds_done)], inc="gp")
            nrm_used = g2
            rmin = pl.op("vector",
                         lambda v, qt=qt: v.tensor_reduce(
                             nr4f[:].unsqueeze(2),
                             nrm2[:].rearrange("p (q j) -> p q j", j=8)[:, :, 0:qt + 1],
                             AX.X, ALU.min),
                         waits=[("gp", g2), ("gp", nr4f_used)], inc="dve")
            gc2 = pl.op("gpsimd", lambda g: g.tensor_copy(nr4i[:], nr4f[:]),
                        waits=[("dve", rmin), ("pe", nr4i_used)], inc="gp")
            nr4f_used = gc2
            ntr = pl.op("tensor",
                        lambda t: t.transpose(p_vtr[0:4, 0:128], nr4i[:], ident_bf[:]),
                        waits=[("gp", gc2), ("dve", vtr_ev)], inc="pe")
            ntr_prev = ntr
            nr4i_used = ntr
            vtr_ev_l = pl.op("vector", lambda v: v.tensor_copy(nr4b[:], p_vtr[0:4, 0:128]),
                             waits=[("pe", ntr)], inc="dve")
            vtr_ev = vtr_ev_l
            ncp = pl.op("sync",
                        lambda s, h=h: s.dma_start(
                            out=Qt[h][64:65, :].rearrange("o (q f) -> o q f", q=4),
                            in_=nr4b[:]),
                        waits=[("dve", vtr_ev_l)], inc="d_cnr", inck=16)

            nkt = (qt + 1) * 4
            for j in range(nkt):
                pb = j % 2
                vv = j - qt * 4

                def s2mm(t, h=h, j=j, pb=pb):
                    return t.matmul(p_s[pb][:], Kst[h][:, j * 128:(j + 1) * 128],
                                    Qt[h][:], start=True, stop=True)
                mm = pl.op("tensor", s2mm,
                           waits=[("d_cnr", ncp), ("act", ps_act[pb]),
                                  ("dve", ps_dve[pb])], inc="pe")
                qt_used = mm
                ex_w = [("pe", mm)]
                if vv >= 0:
                    mk = pl.op("vector",
                               lambda v, pb=pb, vv=vv: v.tensor_add(p_s[pb][:], p_s[pb][:],
                                                                    mkey[vv][:]),
                               waits=[("pe", mm), ("gp", masks_v)], inc="dve")
                    ps_dve[pb] = mk
                    ex_w = [("dve", mk)]
                ex = pl.op("scalar",
                           lambda a, pb=pb: a.activation(Pt[pb][:], p_s[pb][:], AF.Exp,
                                                         scale=0.125),
                           waits=ex_w + [("pe", pt_used[pb])], inc="act")
                ps_act[pb] = ex

                pv = pl.op("tensor",
                           lambda t, h=h, j=j, pb=pb, first=(j == 0), last=(j == nkt - 1):
                           t.matmul(p_o[:], Vst[h][:, j * 65:(j + 1) * 65], Pt[pb][:],
                                    start=first, stop=last),
                           waits=[("act", ex), ("dve", po_ev)], inc="pe")
                pt_used[pb] = pv
            rc = pl.op("vector", lambda v: v.reciprocal(rden[:], p_o[64:65, :]),
                       waits=[("pe", p.v["pe"]), ("d_cden", den_st)], inc="dve")
            den_st = pl.op("sync", lambda s: s.dma_start(out=io["den_scr"], in_=rden[:]),
                           waits=[("dve", rc)], inc="d_cden", inck=16)
            dld = pl.op("sync", lambda s: s.dma_start(
                out=den_bc[:], in_=io["den_scr"].partition_broadcast(64)),
                waits=[("dve", denb_used), ("d_cden", den_st)], inc="d_cdb", inck=16)
            sh = h % 2
            st = pl.op("vector",
                       lambda v, h=h, sh=sh: v.tensor_mul(
                           seq_fm[h // 2][sh * 64:sh * 64 + 64, :], p_o[0:64, :], den_bc[:]),
                       waits=[("d_cdb", dld), ("pe", sf_used[h // 2])], inc="dve")
            denb_used = st
            po_ev = st
        seq_rdy = p.v["dve"]

        gmv = mem.get("gmv") if hp == 0 else None
        gmv_rdy = mem.get("gmv_rdy", 0) if hp == 0 else 0
        if gmv_rdy == -1:
            gmv, gmv_rdy = None, 0
        for oc in range(8):
            pb = oc % 2

            def omm(t, oc=oc, pb=pb, gmv=gmv):
                last_seq = gmv is None
                for r in range(2):
                    i = t.matmul(p_ow[pb][:], outw[r][:, oc * 128:(oc + 1) * 128],
                                 seq_fm[r][:], start=(r == 0), stop=(last_seq and r == 1))
                if gmv is not None:
                    for kc in range(NXC):
                        i = t.matmul(p_ow[pb][:], outw[kc][:, oc * 128:(oc + 1) * 128],
                                     gmv[kc][:], start=False, stop=(kc == NXC - 1))
                return i
            mm = pl.op("tensor", omm,
                       waits=[("dve", max(seq_rdy, gmv_rdy)),
                              ("act", pqkv_ev[pb]),
                              (pow_rd_kind[pb], pow_rd[pb])], inc="pe")
            sf_used[0] = mm
            sf_used[1] = mm

            if hp == 0:
                oe = pl.op("scalar",
                           lambda a, oc=oc, pb=pb: a.activation(ao_t[pb][:], p_ow[pb][:],
                                                                AF.Identity,
                                                                bias=ob_c[:, oc:oc + 1]),
                           waits=[("pe", mm), ("d_ccst", cst_all),
                                  (f"d_cao{pb}", ao_st[pb])], inc="act")
                pow_rd[pb], pow_rd_kind[pb] = oe, "act"
                finw = [("act", oe)]
            else:
                lda = pl.op("sync",
                            lambda s, qt=qt, oc=oc, pb=pb: s.dma_start(
                                out=ap_t[pb][:],
                                in_=io["ap_scr"][oc * 128:(oc + 1) * 128, qt * TT:(qt + 1) * TT]),
                            waits=[("dve", ap_ld[pb])], inc=f"d_cap{pb}", inck=16)
                ad = pl.op("vector",
                           lambda v, pb=pb: v.tensor_add(ao_t[pb][:], ap_t[pb][:], p_ow[pb][:]),
                           waits=[("pe", mm), (f"d_cap{pb}", lda),
                                  (f"d_cao{pb}", ao_st[pb])], inc="dve")
                ap_ld[pb] = ad
                pow_rd[pb], pow_rd_kind[pb] = ad, "dve"
                finw = [("dve", ad)]
            ao_st[pb] = pl.op("sync",
                              lambda s, qt=qt, oc=oc, pb=pb: s.dma_start(
                                  out=io["ap_scr"][oc * 128:(oc + 1) * 128, qt * TT:(qt + 1) * TT],
                                  in_=ao_t[pb][:]),
                              waits=finw, inc=f"d_cao{pb}", inck=16)
        if hp == 0 and not nomem:
            mem["bk"]["gmv_used"] = p.v["pe"]


PAIRS21 = [(a, b) for a in range(6) for b in range(a, 6)]
PAIRS6 = [(0, 1), (0, 2), (0, 3), (1, 2), (1, 3), (2, 3)]


def emit_c_mem_setup(p, pl, sb, ps, io, ident_f32, ident_bf, idv):
    mem = {}
    wstage = sb("m_wst", [128, D], F32)
    mw = {nm: [sb(f"m_{nm}{kc}", [128, 64], BF16) for kc in range(NXC)]
          for nm in ("w1w", "w2w", "w1r", "w2r")}
    stg = 0
    for nm, key in (("w1w", "w1_write"), ("w2w", "w2_write"),
                    ("w1r", "w1_read"), ("w2r", "w2_read")):
        for kc in range(NXC):
            ld = pl.op("sync", lambda s, key=key, kc=kc: s.dma_start(
                out=wstage[:, 0:64], in_=io[key][kc * 128:(kc + 1) * 128, :]),
                waits=[("gp", stg)], inc="d_mwst", inck=16)
            stg = pl.op("gpsimd", lambda g, nm=nm, kc=kc: g.tensor_copy(
                mw[nm][kc][:], wstage[:, 0:64]), waits=[("d_mwst", ld)], inc="gp")
    memv = [sb(f"m_mv{kc}", [128, D], BF16) for kc in range(NXC)]
    for kc in range(NXC):
        ld = pl.op("sync", lambda s, kc=kc: s.dma_start(
            out=wstage[:, 0:D], in_=io["memv_w"][kc * 128:(kc + 1) * 128, :]),
            waits=[("gp", stg)], inc="d_mwst", inck=16)
        stg = pl.op("gpsimd", lambda g, kc=kc: g.tensor_copy(memv[kc][:], wstage[:, 0:D]),
                    waits=[("d_mwst", ld)], inc="gp")
    memg = [sb(f"m_mg{kc}", [128, NHEAD], BF16) for kc in range(NXC)]
    for kc in range(NXC):
        ld = pl.op("sync", lambda s, kc=kc: s.dma_start(
            out=wstage[:, 0:NHEAD], in_=io["memg_w"][kc * 128:(kc + 1) * 128, :]),
            waits=[("gp", stg)], inc="d_mwst", inck=16)
        stg = pl.op("gpsimd", lambda g, kc=kc: g.tensor_copy(memg[kc][:], wstage[:, 0:NHEAD]),
                    waits=[("d_mwst", ld)], inc="gp")
    mem["w_rdy"] = stg
    mvb = sb("m_mvb", [128, 8], F32)
    for kc in range(NXC):
        col_load(pl, mvb[:, kc:kc + 1], io["memv_b"][kc * 128:(kc + 1) * 128], sem="d_mcst")
    mgb = sb("m_mgb", [NHEAD, 1], F32)
    mgld = pl.op("sync", lambda s: s.dma_start(out=mgb[:], in_=io["memg_b"].unsqueeze(1)),
                 inc="d_mcst", inck=16)
    jsm = sb("m_j", [6, 6], F32)
    jsb = sb("m_jb", [6, 6], BF16)
    jblk = sb("m_jblk", [96, 96], BF16)
    jld = pl.op("sync", lambda s: s.dma_start(out=jsm[:], in_=io["J"]), inc="d_mcst", inck=16)
    jz = pl.op("gpsimd", lambda g: g.memset(jblk[:], 0.0), inc="gp")
    jc = pl.op("scalar", lambda a: a.activation(jsb[:], jsm[:], AF.Copy),
               waits=[("d_mcst", p.v["d_mcst"])], inc="act")
    jcp = 0
    for hh in range(NHEAD):
        jcp = pl.op("sync", lambda s, hh=hh: s.dma_start(
            out=jblk[6 * hh:6 * hh + 6, 6 * hh:6 * hh + 6], in_=jsb[:]),
            waits=[("act", jc), ("gp", jz)], inc="d_mj", inck=16)
    mem["jblk"] = jblk
    mem["j_rdy"] = jcp

    dec = sb("m_dec", [NHEAD, 1], F32)
    lng = sb("m_lng", [NHEAD, 1], F32)
    g128r = sb("m_g128", [NHEAD, 1], F32)
    dld = pl.op("sync", lambda s: s.dma_start(out=dec[:], in_=io["decays"].unsqueeze(1)),
                inc="d_mcst", inck=16)
    l1 = pl.op("scalar", lambda a: a.activation(lng[:], dec[:], AF.Ln),
               waits=[("d_mcst", p.v["d_mcst"])], inc="act")
    l2 = pl.op("scalar", lambda a: a.activation(g128r[:], lng[:], AF.Exp, scale=128.0),
               waits=[("act", l1)], inc="act")
    sd1 = pl.op("sync", lambda s: s.dma_start(out=io["mem_scr"][0:1, :], in_=lng[:]),
                waits=[("act", l2)], inc="d_mg", inck=16)
    sd2 = pl.op("sync", lambda s: s.dma_start(out=io["mem_scr"][1:2, :], in_=g128r[:]),
                inc="d_mg", inck=16)
    lng_bc = sb("m_lngb", [128, NHEAD], F32)
    g128_bc = sb("m_g128b", [21, NHEAD], F32)
    ms_bc = sb("m_msb", [128, NHEAD], F32)
    bld = pl.op("sync", lambda s: s.dma_start(
        out=lng_bc[:], in_=io["mem_scr"][0:1, :].partition_broadcast(128)),
        waits=[("d_mg", sd2)], inc="d_mg", inck=16)
    bld2 = pl.op("sync", lambda s: s.dma_start(
        out=g128_bc[:], in_=io["mem_scr"][1:2, :].partition_broadcast(21)),
        inc="d_mg", inck=16)
    msld = pl.op("sync", lambda s: s.dma_start(
        out=ms_bc[:], in_=io["mem_scale"].unsqueeze(0).partition_broadcast(128)),
        inc="d_mg", inck=16)
    iora = sb("m_iora", [128, 1], F32)
    iorev = sb("m_iorev", [128, 1], BF16)
    idif = sb("m_idif", [128, 128], F32)

    def iof(g):
        g.iota(iora[:], pattern=[[0, 1]], base=1, channel_multiplier=1,
               allow_small_or_imprecise_dtypes=True)
        g.iota(iorev[:], pattern=[[0, 1]], base=127, channel_multiplier=-1,
               allow_small_or_imprecise_dtypes=True)
        return g.iota(idif[:], pattern=[[-1, 128]], base=0, channel_multiplier=1,
                      allow_small_or_imprecise_dtypes=True)
    iov = pl.op("gpsimd", iof, inc="gp")
    gpow = sb("m_gpow", [128, NHEAD], F32)
    grev = sb("m_grev", [128, NHEAD], BF16)
    gmask = [sb(f"m_gm{hh}", [128, 128], BF16) for hh in range(NHEAD)]
    gm_f = sb("m_gmf", [128, 128], F32)
    gp1 = pl.op("scalar", lambda a: [a.activation(gpow[:, hh:hh + 1], iora[:], AF.Exp,
                                                  scale=lng_bc[:, hh:hh + 1])
                                     for hh in range(NHEAD)][-1],
                waits=[("gp", iov), ("d_mg", p.v["d_mg"])], inc="act")
    gp2 = pl.op("scalar", lambda a: [a.activation(grev[:, hh:hh + 1], iorev[:], AF.Exp,
                                                  scale=lng_bc[:, hh:hh + 1])
                                     for hh in range(NHEAD)][-1],
                waits=[("act", gp1)], inc="act")
    gmv_l = 0
    for hh in range(NHEAD):
        ge = pl.op("scalar", lambda a, hh=hh: a.activation(gm_f[:], idif[:], AF.Exp,
                                                           scale=lng_bc[:, hh:hh + 1]),
                   waits=[("act", gp2), ("gp", gmv_l)], inc="act")
        gmv_l = pl.op("gpsimd",
                      lambda g, hh=hh: (g.tensor_copy(gmask[hh][:], gm_f[:]),
                                        g.affine_select(out=gmask[hh][:], in_=gmask[hh][:],
                                                        compare_op=ALU.is_gt, fill=0.0,
                                                        base=0, pattern=[[-1, 128]],
                                                        channel_multiplier=1))[-1],
                      waits=[("act", ge)], inc="gp")
    mem.update(gpow=gpow, grev=grev, gmask=gmask, ms_bc=ms_bc, mgb=mgb, mvb=mvb,
               mw=mw, memv=memv, memg=memg, g128_bc=g128_bc,
               consts_rdy=p.v["gp"], consts_act=p.v["act"], cst=p.v["d_mcst"])
    S_all = sb("m_S", [21, NHEAD], F32)
    S_bc = sb("m_Sbc", [128, 21 * NHEAD], F32)
    sz = pl.op("gpsimd", lambda g: (g.memset(S_all[:], 0.0), g.memset(S_bc[:], 0.0))[-1],
               inc="gp")
    mem.update(S_all=S_all, S_bc=S_bc, s_rdy=sz, s_bc_ld=0, s_st=0)
    mem["pj"] = [sb(f"m_pj{i}", [64, TT], BF16) for i in range(4)]
    mem["pjr"] = sb("m_pjr", [64, TT], BF16)
    mem["halo64"] = sb("m_h64", [64, 1], BF16)
    mem["pj_tm"] = [sb(f"m_pjt{i}", [128, 4 * 64], BF16) for i in range(4)]
    mem["Lw"] = sb("m_Lw", [128, 96], F32)
    mem["Lr"] = sb("m_Lr", [128, 96], F32)
    mem["Lsc"] = sb("m_Lsc", [128, 16], F32)
    mem["L2"] = sb("m_L2", [128, 96], F32)
    mem["n2"] = sb("m_n2", [128, NHEAD], F32)
    mem["sn"] = sb("m_sn", [128, NHEAD], F32)
    mem["rn"] = sb("m_rn", [128, NHEAD], F32)
    mem["LwT"] = sb("m_LwT", [96, 128], BF16)
    mem["Lwb"] = sb("m_Lwb", [128, 96], BF16)
    mem["JwT"] = sb("m_JwT", [96, 128], BF16)
    mem["Jw_tm"] = sb("m_Jwtm", [128, 96], F32)
    mem["phi"] = sb("m_phi", [128, 16 * DH], BF16)
    mem["psi"] = sb("m_psi", [128, 16 * DH], BF16)
    mem["phif"] = [sb(f"m_phif{i}", [128, 128], BF16) for i in range(8)]
    mem["psif"] = [sb(f"m_psif{i}", [128, 128], BF16) for i in range(8)]
    mem["Mm"] = sb("m_Mm", [128, 128], F32)
    mem["intra"] = sb("m_intra", [128, NHEAD], F32)
    mem["inter"] = sb("m_intert", [128, 21 * NHEAD], F32)
    mem["score"] = sb("m_score", [128, NHEAD], F32)
    mem["scb"] = [sb(f"m_scb{ch}", [128, NHEAD], F32) for ch in range(4)]
    mem["gate_f"] = sb("m_gatef", [NHEAD, TT], BF16)
    mem["gate_tm"] = sb("m_gatetm", [128, NHEAD], F32)
    mem["gcol"] = sb("m_gcol", [128, 4], F32)
    mem["gcol_b"] = sb("m_gcolb", [128, 4], BF16)
    mem["g4"] = sb("m_g4", [4, 128], BF16)
    mem["g_bc"] = sb("m_gbc", [128, TT], BF16)
    mem["mv_t"] = sb("m_mvt", [128, TT], F32)
    mem["gmv"] = [sb(f"m_gmv{kc}", [128, TT], BF16) for kc in range(NXC)]
    mem["p_t"] = ps("m_pt", [128, 128], F32)
    mem["p_tb"] = ps("m_ptb", [128, 128], BF16)
    hz = pl.op("gpsimd", lambda g: (g.memset(mem["halo64"][:], 0.0),
                                    g.memset(mem["phi"][:], 0.0),
                                    g.memset(mem["psi"][:], 0.0))[-1], inc="gp")
    mem["halo_z"] = hz
    mem["bk"] = {}
    return mem


def emit_c_mem_qtile(p, pl, sb, ps, io, mem, qt, xn_fm, xn_rdy):
    import os
    LVL = int(os.environ.get("C_MEMLVL", "9"))
    bk = mem["bk"]
    p_m, p_t, p_tb = mem["p_m"], mem["p_t"], mem["p_tb"]

    sh_rdy = xn_rdy
    for ip, nm in enumerate(("w1w", "w2w", "w1r", "w2r")):
        pb = ip % 2

        def pmm(t, nm=nm, pb=pb):
            for kc in range(NXC):
                i = t.matmul(p_m[pb][0:64, :], mem["mw"][nm][kc][:], xn_fm[kc][:],
                             start=(kc == 0), stop=(kc == NXC - 1))
            return i
        mm = pl.op("tensor", pmm,
                   waits=[("dve", max(sh_rdy, mem["ps_dve"][pb])), ("gp", mem["w_rdy"]),
                          ("act", max(bk.get(f"pm_ev{pb}", 0), mem["ps_act"][pb]))], inc="pe")
        dst = mem["pjr"] if ip == 0 else mem["pj"][ip]
        ev = pl.op("scalar", lambda a, dst=dst, pb=pb: a.activation(
            dst[:], p_m[pb][0:64, :], AF.Copy),
            waits=[("pe", mm), ("dve", bk.get("pj_used", 0))], inc="act")
        bk[f"pm_ev{pb}"] = ev
        if ip == 0:
            def shf(v):
                v.tensor_copy(mem["pj"][0][:, 1:TT], mem["pjr"][:, 0:TT - 1])
                v.tensor_copy(mem["pj"][0][:, 0:1], mem["halo64"][:])
                return v.tensor_copy(mem["halo64"][:], mem["pjr"][:, TT - 1:TT])
            pl.op("vector", shf, waits=[("act", ev), ("gp", mem["halo_z"]),
                                        ("dve", bk.get("pj_used", 0))], inc="dve")
    pj_rdy_a = p.v["act"]
    pj_rdy_d = p.v["dve"]
    if LVL < 3:
        mem["gmv_rdy"] = -1
        return

    for ch in range(4):
        for ip in range(4):
            tr = pl.op("tensor", lambda t, ip=ip, ch=ch: t.transpose(
                p_tb[0:128, 0:64], mem["pj"][ip][:, ch * 128:(ch + 1) * 128],
                mem["idb"][0:64, 0:64]),
                waits=[("act", max(pj_rdy_a, bk.get("ptb_ev", 0))),
                       ("dve", max(pj_rdy_d, bk.get("ptb_ev2", 0)))],
                inc="pe")
            ev = pl.op("vector", lambda v, ip=ip, ch=ch: v.tensor_copy(
                mem["pj_tm"][ch][:, ip * 64:(ip + 1) * 64], p_tb[0:128, 0:64]),
                waits=[("pe", tr)], inc="dve")
            bk["ptb_ev2"] = ev
            bk["pt_ev"] = ev
    bk["pj_used"] = p.v["dve"]
    if LVL < 4:
        mem["gmv_rdy"] = -1
        return

    per_chunk_mem(p, pl, io, mem, qt, LVL)
    if LVL < 6:
        mem["gmv_rdy"] = -1
        return

    gm = pl.op("tensor", lambda t: [t.matmul(p_m[0][0:NHEAD, :], mem["memg"][kc][:],
                                             xn_fm[kc][:], start=(kc == 0),
                                             stop=(kc == NXC - 1))
                                    for kc in range(NXC)][-1],
               waits=[("dve", max(xn_rdy, mem["ps_dve"][0])), ("gp", mem["w_rdy"]),
                      ("act", max(bk.get("gm_ev", 0), mem["ps_act"][0]))], inc="pe")
    ge = pl.op("scalar", lambda a: a.activation(mem["gate_f"][:], p_m[0][0:NHEAD, :],
                                                AF.Sigmoid, bias=mem["mgb"][:]),
               waits=[("pe", gm), ("d_mcst", mem["cst"]), ("pe", bk.get("gatef_used", 0))],
               inc="act")
    bk["gm_ev"] = ge

    for ch in range(4):
        gtr = pl.op("tensor", lambda t, ch=ch: t.transpose(
            p_tb[0:128, 0:NHEAD], mem["gate_f"][:, ch * 128:(ch + 1) * 128],
            mem["idb"][0:NHEAD, 0:NHEAD]),
            waits=[("act", ge), ("dve", bk.get("gt_ev", 0))], inc="pe")
        bk["gatef_used"] = gtr
        gev = pl.op("vector", lambda v, ch=ch: v.tensor_copy(
            mem["gate_tm"][:], p_tb[0:128, 0:NHEAD]),
            waits=[("pe", gtr), ("dve", bk.get("gatetm_used", 0))], inc="dve")
        t1 = pl.op("vector", lambda v, ch=ch: v.tensor_mul(
            mem["score"][:], bk[f"score{ch}"][0], mem["ms_bc"][:]),
            waits=[("dve", max(gev, bk[f"score{ch}"][1])),
                   ("d_mg", p.v["d_mg"])], inc="dve")
        bk[f"scb_used{ch}"] = t1
        t2 = pl.op("scalar", lambda a: a.activation(mem["score"][:], mem["score"][:],
                                                    AF.Sigmoid),
                   waits=[("dve", t1)], inc="act")
        bk["gsc1"] = t2
        t3 = pl.op("vector", lambda v: v.tensor_mul(mem["score"][:], mem["score"][:],
                                                    mem["gate_tm"][:]),
                   waits=[("act", t2)], inc="dve")
        bk["gatetm_used"] = t3
        t4 = pl.op("vector", lambda v, ch=ch: v.tensor_reduce(
            mem["gcol"][:, ch:ch + 1].unsqueeze(2), mem["score"][:].unsqueeze(1),
            AX.X, ALU.add),
            waits=[("dve", t3), ("pe", bk.get("gcol_used", 0))], inc="dve")
        bk["gt_ev"] = t4
        bk["score_used"] = t4
    gsc = pl.op("vector", lambda v: v.tensor_scalar_mul(mem["gcol_b"][:], mem["gcol"][:],
                                                        1.0 / NHEAD),
                waits=[("dve", p.v["dve"])], inc="dve")
    gtr2 = pl.op("tensor", lambda t: t.transpose(p_tb[0:4, 0:128], mem["gcol_b"][:],
                                                 mem["idb"][:]),
                 waits=[("dve", gsc)], inc="pe")
    bk["gcol_used"] = gtr2
    gv2 = pl.op("vector", lambda v: v.tensor_copy(mem["g4"][:], p_tb[0:4, 0:128]),
                waits=[("pe", gtr2), ("d_mgs", bk.get("g4_st", 0))], inc="dve")
    g4st = pl.op("sync", lambda s: s.dma_start(out=io["g_scr"], in_=mem["g4"][:]),
                 waits=[("dve", gv2)], inc="d_mgs", inck=16)
    bk["g4_st"] = g4st
    gbld = pl.op("sync", lambda s: s.dma_start(
        out=mem["g_bc"][:], in_=io["g_scr"].rearrange("q f -> (q f)").unsqueeze(0).partition_broadcast(128)),
        waits=[("d_mgs", g4st), ("dve", bk.get("gbc_used", 0))], inc="d_mgb", inck=16)

    for kc in range(NXC):
        pb = kc % 2
        mv = pl.op("tensor", lambda t, kc=kc, pb=pb: [
            t.matmul(p_m[pb][:], mem["memv"][k2][:, kc * 128:(kc + 1) * 128], xn_fm[k2][:],
                     start=(k2 == 0), stop=(k2 == NXC - 1)) for k2 in range(NXC)][-1],
            waits=[("dve", max(xn_rdy, mem["ps_dve"][pb])), ("gp", mem["w_rdy"]),
                   ("act", max(bk.get(f"pm_ev{pb}", 0), mem["ps_act"][pb]))], inc="pe")
        mve = pl.op("scalar", lambda a, kc=kc, pb=pb: a.activation(
            mem["mv_t"][:], p_m[pb][:], AF.Identity, bias=mem["mvb"][:, kc:kc + 1]),
            waits=[("pe", mv), ("d_mcst", mem["cst"]), ("dve", bk.get("mvt_used", 0))],
            inc="act")
        bk[f"pm_ev{pb}"] = mve
        gm2 = pl.op("vector", lambda v, kc=kc: v.tensor_mul(
            mem["gmv"][kc][:], mem["mv_t"][:], mem["g_bc"][:]),
            waits=[("act", mve), ("d_mgb", gbld), ("pe", bk.get("gmv_used", 0))], inc="dve")
        bk["mvt_used"] = gm2
    bk["gbc_used"] = p.v["dve"]
    mem["gmv_rdy"] = p.v["dve"]


def per_chunk_mem(p, pl, io, mem, qt, LVL=9):
    bk = mem["bk"]
    p_t, p_tb = mem["p_t"], mem["p_tb"]
    for ch in range(4):
        pj = mem["pj_tm"][ch]
        for which, (o1, o2), Ldst in (("w", (0, 1), mem["Lw"]), ("r", (2, 3), mem["Lr"])):
            v4 = pj[:].rearrange("p (o h c) -> p o h c", o=4, c=4)
            Lv = Ldst[:].rearrange("p (h k) -> p h k", k=6)
            L2v = mem["L2"][:].rearrange("p (h k) -> p h k", k=6)
            m1l = 0
            for k, (i6, j6) in enumerate(PAIRS6):
                m1l = pl.op("vector", lambda v, i6=i6, j6=j6, k=k: v.tensor_mul(
                    Lv[:, :, k], v4[:, o1, :, i6], v4[:, o2, :, j6]),
                    waits=[("dve", bk.get("L_used", 0)), ("gp", bk.get("L2g", 0))]
                    if k == 0 else [], inc="dve")
            m2l = 0
            for k, (i6, j6) in enumerate(PAIRS6):
                m2l = pl.op("gpsimd", lambda g, i6=i6, j6=j6, k=k: g.tensor_mul(
                    L2v[:, :, k], v4[:, o1, :, j6], v4[:, o2, :, i6]),
                    waits=[("dve", bk.get("pt_ev", 0)), ("act", bk.get("L2a", 0))]
                    if k == 0 else [], inc="gp")
            m3l = 0
            for k in range(6):
                m3l = pl.op("vector", lambda v, k=k: v.tensor_sub(
                    Lv[:, :, k], Lv[:, :, k], L2v[:, :, k]),
                    waits=[("gp", m2l), ("dve", m1l)] if k == 0 else [], inc="dve")
            sq = pl.op("scalar", lambda a, Ldst=Ldst: a.activation(mem["L2"][:], Ldst[:],
                                                                   AF.Square),
                       waits=[("dve", m3l)], inc="act")
            bk["L2a"] = sq
            n2 = pl.op("vector", lambda v: v.tensor_reduce(
                mem["n2"][:].unsqueeze(2), mem["L2"][:].rearrange("p (h k) -> p h k", k=6),
                AX.X, ALU.add), waits=[("act", sq)], inc="dve")
            sn = pl.op("scalar", lambda a: a.activation(mem["sn"][:], mem["n2"][:], AF.Sqrt),
                       waits=[("dve", n2)], inc="act")
            mx = pl.op("gpsimd", lambda g: g.tensor_scalar_max(mem["sn"][:], mem["sn"][:],
                                                               1e-12),
                       waits=[("act", sn)], inc="gp")
            bk["L2g"] = mx
            rn = pl.op("vector", lambda v: v.reciprocal(mem["rn"][:], mem["sn"][:]),
                       waits=[("gp", mx)], inc="dve")
            nm = pl.op("vector", lambda v, Lv=Lv: v.tensor_mul(
                Lv, Lv, mem["rn"][:].unsqueeze(2).broadcast_to([128, NHEAD, 6])),
                waits=[("dve", rn)], inc="dve")
            bk["L_used"] = nm

        if LVL < 5:
            continue
        cb = pl.op("vector", lambda v: v.tensor_copy(mem["Lwb"][:], mem["Lw"][:]),
                   waits=[("dve", bk["L_used"]), ("pe", bk.get("lwb_used", 0))], inc="dve")
        tr1 = pl.op("tensor", lambda t: t.transpose(p_tb[0:96, 0:128], mem["Lwb"][:],
                                                    mem["idb"][:]),
                    waits=[("dve", cb)], inc="pe")
        bk["lwb_used"] = tr1
        e1 = pl.op("scalar", lambda a: a.activation(mem["LwT"][:], p_tb[0:96, 0:128], AF.Copy),
                   waits=[("pe", tr1)], inc="act")
        jmm = pl.op("tensor", lambda t: t.matmul(p_t[0:96, 0:128], mem["jblk"][:],
                                                 mem["LwT"][:], start=True, stop=True),
                    waits=[("act", e1), ("d_mj", mem["j_rdy"]),
                           ("dve", bk.get("mm_ev", 0))], inc="pe")
        e2 = pl.op("scalar", lambda a: a.activation(mem["JwT"][:], p_t[0:96, 0:128], AF.Copy),
                   waits=[("pe", jmm)], inc="act")
        tr2 = pl.op("tensor", lambda t: t.transpose(p_tb[0:128, 0:96], mem["JwT"][:],
                                                    mem["idb"][0:96, 0:96]),
                    waits=[("act", e2)], inc="pe")
        e3 = pl.op("scalar", lambda a: a.activation(mem["Jw_tm"][:], p_tb[0:128, 0:96],
                                                    AF.Copy),
                   waits=[("pe", tr2), ("dve", bk.get("jwtm_used", 0))], inc="act")
        bk["ptb_ev"] = e3

        ph_w = [("act", e3), ("gp", mem["halo_z"]), ("pe", bk.get("phi_used", 0))]
        last = 0
        for k, (p6, q6) in enumerate(PAIRS21):
            ph = pl.op("vector", lambda v, k=k, p6=p6, q6=q6: v.tensor_mul(
                mem["phi"][:].rearrange("p (h c) -> p h c", c=DH)[:, :, k],
                mem["Lr"][:].rearrange("p (h c) -> p h c", c=6)[:, :, p6],
                mem["Lr"][:].rearrange("p (h c) -> p h c", c=6)[:, :, q6]),
                waits=ph_w if k == 0 else [], inc="dve")
            psx = pl.op("vector", lambda v, k=k, p6=p6, q6=q6: v.tensor_mul(
                mem["psi"][:].rearrange("p (h c) -> p h c", c=DH)[:, :, k],
                mem["Jw_tm"][:].rearrange("p (h c) -> p h c", c=6)[:, :, p6],
                mem["Jw_tm"][:].rearrange("p (h c) -> p h c", c=6)[:, :, q6]),
                inc="dve")
            last = psx
        bk["jwtm_used"] = last
        s1 = pl.op("vector", lambda v: v.tensor_scalar_mul(
            mem["psi"][:].rearrange("p (h c) -> p h c", c=DH)[:, :, 0:21],
            mem["psi"][:].rearrange("p (h c) -> p h c", c=DH)[:, :, 0:21], 2.0),
            waits=[("dve", last)], inc="dve")
        dk = [k for k, (p6, q6) in enumerate(PAIRS21) if p6 == q6]
        s2 = 0
        for k in dk:
            s2 = pl.op("vector", lambda v, k=k: v.tensor_scalar_mul(
                mem["psi"][:].rearrange("p (h c) -> p h c", c=DH)[:, :, k],
                mem["psi"][:].rearrange("p (h c) -> p h c", c=DH)[:, :, k], 0.5),
                inc="dve")

        for t8 in range(8):
            for which, src, dstl in (("p", mem["phi"], mem["phif"]),
                                     ("s", mem["psi"], mem["psif"])):
                tr = pl.op("tensor", lambda t, t8=t8, src=src: t.transpose(
                    p_tb[0:128, 0:128], src[:, t8 * 128:(t8 + 1) * 128], mem["idb"][:]),
                    waits=[("dve", p.v["dve"]), ("act", bk.get("ptb_ev", 0))], inc="pe")
                ev = pl.op("scalar", lambda a, t8=t8, dstl=dstl: a.activation(
                    dstl[t8][:], p_tb[0:128, 0:128], AF.Copy),
                    waits=[("pe", tr), ("pe", bk.get("phif_used", 0))], inc="act")
                bk["ptb_ev"] = ev
        ff_rdy = p.v["act"]

        for hh in range(NHEAD):
            t8 = hh // 2
            b0 = (hh % 2) * 64
            m1 = pl.op("tensor", lambda t, t8=t8, b0=b0: t.matmul(
                p_t[:], mem["phif"][t8][b0:b0 + 21, :], mem["psif"][t8][b0:b0 + 21, :],
                start=True, stop=True),
                waits=[("act", ff_rdy), ("dve", bk.get("mm_ev", 0))], inc="pe")
            bk["phif_used"] = m1
            mk = pl.op("vector", lambda v, hh=hh: v.tensor_mul(mem["Mm"][:], p_t[:],
                                                               mem["gmask"][hh][:]),
                       waits=[("pe", m1), ("gp", mem["consts_rdy"])], inc="dve")
            rs = pl.op("vector", lambda v, hh=hh: v.tensor_reduce(
                mem["intra"][:, hh:hh + 1].unsqueeze(2), mem["Mm"][:].unsqueeze(1),
                AX.X, ALU.add), waits=[("dve", mk)], inc="dve")
            bk["mm_ev"] = rs
        it1 = pl.op("vector", lambda v: v.tensor_mul(
            mem["inter"][:].rearrange("p (c h) -> p h c", c=21),
            mem["phi"][:].rearrange("p (h c) -> p h c", c=DH)[:, :, 0:21],
            mem["S_bc"][:].rearrange("p (c h) -> p h c", c=21)),
            waits=[("dve", p.v["dve"]), ("gp", mem["s_rdy"]),
                   ("d_msb", mem.get("s_bc_ld", 0))], inc="dve")
        it2 = pl.op("vector", lambda v: v.tensor_reduce(
            mem["score"][:].unsqueeze(2),
            mem["inter"][:].rearrange("p (c h) -> p h c", c=21), AX.X, ALU.add),
            waits=[("dve", max(it1, bk.get("score_used", 0)))], inc="dve")
        sc1 = pl.op("vector", lambda v: v.tensor_mul(mem["score"][:], mem["score"][:],
                                                     mem["gpow"][:]),
                    waits=[("dve", it2), ("act", mem["consts_act"])], inc="dve")
        sc2 = pl.op("vector", lambda v: v.tensor_add(mem["score"][:], mem["score"][:],
                                                     mem["intra"][:]),
                    inc="dve")
        scc = pl.op("vector", lambda v, ch=ch: v.tensor_copy(mem["scb"][ch][:],
                                                             mem["score"][:]),
                    waits=[("dve", sc2), ("dve", bk.get(f"scb_used{ch}", 0))], inc="dve")
        bk[f"score{ch}"] = (mem["scb"][ch][:], scc)

        for hh in range(NHEAD):
            t8 = hh // 2
            b0 = (hh % 2) * 64
            um = pl.op("tensor", lambda t, hh=hh: t.matmul(
                p_t[0:21, 0:1], mem["psi"][:, hh * DH:hh * DH + 21],
                mem["grev"][:, hh:hh + 1], start=True, stop=True),
                waits=[("dve", p.v["dve"]), ("act", mem["consts_act"])], inc="pe")
            us = pl.op("vector", lambda v, hh=hh: v.scalar_tensor_tensor(
                mem["S_all"][:, hh:hh + 1], mem["S_all"][:, hh:hh + 1],
                mem["g128_bc"][:, hh:hh + 1], p_t[0:21, 0:1], ALU.mult, ALU.add),
                waits=[("pe", um), ("gp", mem["s_rdy"]), ("d_msb", mem.get("s_bc_ld", 0))],
                inc="dve")
            bk["sup"] = us
        sst = pl.op("sync", lambda s: s.dma_start(
            out=io["s_scr"], in_=mem["S_all"][:]),
            waits=[("dve", p.v["dve"]), ("d_msb", mem.get("s_bc_ld", 0))], inc="d_mss", inck=16)
        sld = pl.op("sync", lambda s: s.dma_start(
            out=mem["S_bc"][:], in_=io["s_scr"].rearrange("f h -> (f h)").unsqueeze(0).partition_broadcast(128)),
            waits=[("d_mss", sst)], inc="d_msb", inck=16)
        mem["s_bc_ld"] = sld
def emit_e(p, pl, sb, ps, io):
    ET = 256
    ident_f32 = sb("eidf", [128, 128], F32)
    ident_bf = sb("eidb", [128, 128], BF16)
    idv = make_ident(pl, ident_f32, ident_bf)

    w1T = [sb(f"e_w1T{kc}", [128, 4096], BF16) for kc in range(NXC)]
    w2T = [sb(f"e_w2T{hc}", [128, D], BF16) for hc in range(32)]
    pst = [ps(f"e_pst{i}", [128, 128], F32) for i in range(2)]
    evb = [0, 0]
    trs = [0, 0]
    with ExitStack() as wstk:
        wldsh = [wstk.enter_context(p.nc.sbuf_tensor(f"e_wld{i}", [128, 4096], F32))
                 for i in range(2)]
        prep_lhsT(pl, sb, ps, io["W1"], 4096, D, w1T, ident_f32, "ew1", pst, evb,
                  wld=wldsh, tr_state=trs)
        prep_lhsT(pl, sb, ps, io["W2"], D, 4096, w2T, ident_f32, "ew2", pst, evb,
                  wld=wldsh, tr_state=trs)
    b1c = sb("e_b1c", [128, 32], F32)
    b2c = sb("e_b2c", [128, 8], F32)
    for hc in range(32):
        col_load(pl, b1c[:, hc:hc + 1], io["b1"][hc * 128:(hc + 1) * 128], sem="d_ec")
    for oc in range(8):
        col_load(pl, b2c[:, oc:oc + 1], io["b2"][oc * 128:(oc + 1) * 128], sem="d_ec")
    sc4 = sb("e_sc4", [4, 1], F32)
    sc16 = sb("e_sc16", [16, 1], F32)
    pl.op("sync", lambda s: s.dma_start(out=sc4[:], in_=io["scale"].unsqueeze(0).partition_broadcast(4)),
          inc="d_ec", inck=16)
    ldec = pl.op("sync", lambda s: s.dma_start(out=sc16[:], in_=io["scale"].unsqueeze(0).partition_broadcast(16)),
                 inc="d_ec", inck=16)

    xn_fm = [sb(f"e_xn{kc}", [128, ET], BF16) for kc in range(NXC)]
    p_sm = ps("e_psm", [128, 128], BF16)
    lnt = LNT(p, pl, sb, ps, "el", nslots=2, ptr=[p_sm])

    o4aT = [sb(f"e_o4a{kc}", [128, 4], BF16) for kc in range(NXC)]
    o4bT = [sb(f"e_o4b{kc}", [128, 4], BF16) for kc in range(NXC)]
    Eh = sb("e_Eh", [4, 4096], BF16)
    E2f = sb("e_E2f", [16, D], BF16)
    h_bf = [sb(f"e_h{hc}", [128, ET], BF16) for hc in range(32)]
    p1s = sb("e_p1s", [4, ET], BF16)
    p4q = sb("e_p4q", [4, ET], BF16)
    Pf = sb("e_Pf", [16, ET], BF16)
    o_t = [sb(f"e_o{i}", [128, ET], F32) for i in range(2)]
    ap_t = [sb(f"e_ap{i}", [128, ET], F32) for i in range(2)]
    ott = [sb(f"e_ott{i}", [128, 128], F32) for i in range(2)]
    p_h = [ps(f"e_ph{i}", [128, ET], F32) for i in range(2)]
    p_p = ps("e_pp", [4, ET], F32)
    p_tr = ps("e_ptr", [128, 128], F32)

    ltstk = ExitStack()
    sblt = lambda n, s, dt=BF16: ltstk.enter_context(p.nc.sbuf_tensor(n, s, dt))
    lt_tm = sblt("e_lttm", [128, D])
    lnt.emit(io["x1_scr"][0:128, :], xn_fm, 0, ident_bf, idv, tm_out=lt_tm[:])
    lt_rdy = p.v["dve"]
    evna = sblt("e_evna", [4, D])
    evnb = sblt("e_evnb", [4, D])
    odda = sblt("e_odda", [4, D])
    oddb = sblt("e_oddb", [4, D])
    e4r = sblt("e_e4r", [4, D])
    rowld = 0
    for ii in range(8):
        ev_dst = evna[ii:ii + 1, :] if ii < 4 else evnb[ii - 4:ii - 3, :]
        od_dst = odda[ii:ii + 1, :] if ii < 4 else oddb[ii - 4:ii - 3, :]
        pl.op("sync", lambda s, ii=ii, d=ev_dst: s.dma_start(
            out=d, in_=lt_tm[1 + 2 * ii:2 + 2 * ii, :]),
            waits=[("dve", lt_rdy)], inc="d_elt", inck=16)
        rowld = pl.op("sync", lambda s, ii=ii, d=od_dst: s.dma_start(
            out=d, in_=lt_tm[2 + 2 * ii:3 + 2 * ii, :]),
            inc="d_elt", inck=16)

    sm_ev = lnt.ptr_ev[0]
    for kc in range(NXC):
        for dst, srcr in ((o4aT, odda), (o4bT, oddb)):
            tr = pl.op("tensor", lambda t, kc=kc, srcr=srcr: t.transpose(
                p_sm[0:128, 0:4], srcr[:, kc * 128:(kc + 1) * 128],
                ident_bf[0:4, 0:4]),
                waits=[("d_elt", rowld), ("dve", sm_ev), ("gp", idv)], inc="pe")
            sm_ev = pl.op("vector", lambda v, kc=kc, dst=dst: v.tensor_copy(
                dst[kc][:], p_sm[0:128, 0:4]), waits=[("pe", tr)], inc="dve")

    ez = pl.op("gpsimd", lambda g: (g.memset(Eh[:], 0.0), g.memset(E2f[:], 0.0))[-1],
               inc="gp")
    eh_ld = 0
    for ii in range(4):
        eh_ld = pl.op("sync", lambda s, ii=ii: s.dma_start(
            out=Eh[ii:ii + 1, ii * D:(ii + 1) * D], in_=evna[ii:ii + 1, :]),
            waits=[("d_elt", rowld), ("gp", ez)], inc="d_eE", inck=16)
    e4c = pl.op("vector", lambda v: v.tensor_copy(
        e4r[:].rearrange("p (q a) -> p q a", q=4),
        evnb[:].rearrange("p (a q) -> p q a", q=4)),
        waits=[("d_elt", rowld)], inc="dve")
    for ii in range(4):
        for qq in range(4):
            eh_ld = pl.op("sync", lambda s, ii=ii, qq=qq: s.dma_start(
                out=E2f[4 * qq + ii:4 * qq + ii + 1, ii * 256:(ii + 1) * 256],
                in_=e4r[ii:ii + 1, qq * 256:(qq + 1) * 256]),
                waits=[("dve", e4c), ("gp", ez)], inc="d_eE", inck=16)
    econ_rdy = eh_ld
    ltstk.close()

    x1cp = pl.op("sync", lambda s: s.dma_start(out=io["out"], in_=io["x1_scr"]),
                 inc="d_ex", inck=16)

    ph_ev = [max(evb), max(evb)]
    hbf_used = [0] * 32
    pp_ev = 0
    p1_used = 0
    pf_used = 0
    ap_ld = [0, 0]
    ot_used = [0, 0]
    fe_of = [0, 0]
    otm_st = [0, 0]
    otr_ev = 0

    NET = T // ET
    NPI = ET // 128
    for qt in range(NET):
        lnt.ptr_ev[0] = max(lnt.ptr_ev[0], sm_ev)
        pis = range(NPI) if qt > 0 else range(1, NPI)
        for pi in pis:
            lnt.emit(io["x1_scr"][(qt * NPI + pi) * 128:(qt * NPI + pi + 1) * 128, :],
                     xn_fm, pi * 128, ident_bf, idv)
        xn_rdy = p.v["dve"]

        pm = pl.op("tensor", lambda t: [t.matmul(p_p[:], o4aT[kc][:], xn_fm[kc][:],
                                                 start=(kc == 0), stop=(kc == NXC - 1))
                                        for kc in range(NXC)][-1],
                   waits=[("dve", max(xn_rdy, sm_ev)), ("act", pp_ev)], inc="pe")
        pe1 = pl.op("scalar", lambda a: a.activation(p1s[:], p_p[:], AF.Identity,
                                                     scale=sc4[:]),
                    waits=[("pe", max(pm, p1_used)), ("d_ec", ldec)], inc="act")
        pp_ev = pe1

        for hc in range(32):
            pb = hc % 2

            def hmm(t, hc=hc, pb=pb):
                for kc in range(NXC):
                    t.matmul(p_h[pb][:], w1T[kc][:, hc * 128:(hc + 1) * 128],
                             xn_fm[kc][:], start=(kc == 0), stop=False)
                return t.matmul(p_h[pb][:], Eh[:, hc * 128:(hc + 1) * 128],
                                p1s[:], start=False, stop=True)
            mm = pl.op("tensor", hmm,
                       waits=[("dve", xn_rdy), ("act", max(ph_ev[pb], pe1)),
                              ("d_eE", econ_rdy),
                              ("pe", hbf_used[hc])], inc="pe")
            p1_used = mm
            ev = pl.op("scalar", lambda a, hc=hc, pb=pb: a.activation(
                h_bf[hc][:], p_h[pb][:], AF.Gelu, bias=b1c[:, hc:hc + 1]),
                waits=[("pe", mm), ("d_ec", ldec)], inc="act")
            ph_ev[pb] = ev
        h_rdy = p.v["act"]

        pf_dma = 0
        for q in range(4):
            pm2 = pl.op("tensor", lambda t, q=q: [
                t.matmul(p_p[:], o4bT[kc][:], h_bf[q * 8 + kc][:],
                         start=(kc == 0), stop=(kc == NXC - 1)) for kc in range(NXC)][-1],
                waits=[("act", max(h_rdy, pp_ev))], inc="pe")
            pe2 = pl.op("scalar", lambda a, q=q: a.activation(
                p4q[:], p_p[:], AF.Identity, scale=sc4[:]),
                waits=[("pe", pm2), ("d_epf", pf_dma)], inc="act")
            pp_ev = pe2
            pf_dma = pl.op("sync", lambda s, q=q: s.dma_start(
                out=Pf[4 * q:4 * q + 4, :], in_=p4q[:]),
                waits=[("act", pe2), ("pe", pf_used)], inc="d_epf", inck=16)

        for oc in range(8):
            pb = oc % 2

            def fmm(t, oc=oc, pb=pb):
                for hc in range(32):
                    t.matmul(p_h[pb][:], w2T[hc][:, oc * 128:(oc + 1) * 128],
                             h_bf[hc][:], start=(hc == 0), stop=False)
                return t.matmul(p_h[pb][:], E2f[:, oc * 128:(oc + 1) * 128], Pf[:],
                                start=False, stop=True)
            mm = pl.op("tensor", fmm,
                       waits=[("act", max(h_rdy, fe_of[pb])), ("d_epf", pf_dma),
                              ("d_eE", econ_rdy)], inc="pe")
            pf_used = mm
            for hc in range(32):
                hbf_used[hc] = mm

            lda = pl.op("sync", lambda s, qt=qt, oc=oc, pb=pb: s.dma_start(
                out=ap_t[pb][:],
                in_=io["ap_scr"][oc * 128:(oc + 1) * 128, qt * ET:(qt + 1) * ET]),
                waits=[("dve", ap_ld[pb])], inc=f"d_eap{pb}", inck=16)
            fe = pl.op("scalar", lambda a, oc=oc, pb=pb: a.activation(
                o_t[pb][:], p_h[pb][:], AF.Identity, bias=b2c[:, oc:oc + 1]),
                waits=[("pe", max(mm, ot_used[pb])), ("d_ec", ldec)], inc="act")
            fe_of[pb] = fe
            ad = pl.op("vector", lambda v, pb=pb: v.tensor_add(o_t[pb][:], o_t[pb][:],
                                                               ap_t[pb][:]),
                       waits=[("act", fe), (f"d_eap{pb}", lda)], inc="dve")
            ap_ld[pb] = ad
            for pi in range(NPI):
                b2i = pi % 2
                tr = pl.op("tensor", lambda t, oc=oc, pb=pb, pi=pi: t.transpose(
                    p_tr[:], o_t[pb][:, pi * 128:(pi + 1) * 128], ident_f32[:]),
                    waits=[("dve", max(ad, otr_ev))], inc="pe")
                ot_used[pb] = tr
                otr_ev = pl.op("vector", lambda v, b2i=b2i: v.tensor_copy(
                    ott[b2i][:], p_tr[:]),
                    waits=[("pe", tr), (f"d_eot{b2i}", otm_st[b2i])], inc="dve")
                otm_st[b2i] = pl.op("gpsimd",
                                    lambda g, qt=qt, oc=oc, pi=pi, b2i=b2i: g.dma_start(
                                        out=io["out"][(qt * NPI + pi) * 128:(qt * NPI + pi + 1) * 128,
                                                      oc * 128:(oc + 1) * 128],
                                        in_=ott[b2i][:], accum_op=ALU.add),
                                    waits=[("dve", otr_ev), ("d_ex", x1cp)],
                                    inc=f"d_eot{b2i}", inck=16)


def build_e():
    nc = bass.Bass("TRN2", target_bir_lowering=False)
    io = {}
    io["x1_scr"] = nc.dram_tensor("x1_scr", [T, D], F32, kind="ExternalInput").ap()
    io["ap_scr"] = nc.dram_tensor("ap_scr", [D, T], F32, kind="ExternalInput").ap()
    io["W1"] = nc.dram_tensor("W1", [4096, D], F32, kind="ExternalInput").ap()
    io["b1"] = nc.dram_tensor("b1", [4096], F32, kind="ExternalInput").ap()
    io["W2"] = nc.dram_tensor("W2", [D, 4096], F32, kind="ExternalInput").ap()
    io["b2"] = nc.dram_tensor("b2", [D], F32, kind="ExternalInput").ap()
    io["scale"] = nc.dram_tensor("scale", [1], F32, kind="ExternalInput").ap()
    io["out"] = nc.dram_tensor("out", [T, D], F32, kind="ExternalOutput").ap()
    plans = []
    with ExitStack() as ctx:
        p = Prog(nc, ctx)
        plans.append(_phase(nc, p, emit_e, io))
    check_plans(plans)
    return nc


def _phase(nc, p, emit, io):
    with ExitStack() as c:
        sb = lambda n, s, dt=F32: c.enter_context(nc.sbuf_tensor(n, s, dt))
        ps_ = lambda n, s, dt=F32: c.enter_context(nc.psum_tensor(n, s, dt))
        pl = Plan(p)
        emit(p, pl, sb, ps_, io)
        pl.barrier()
        pl.realize(nc)
    return pl


def build_a2a3():
    nc = bass.Bass("TRN2", target_bir_lowering=False)
    io = {}
    io["m_scr"] = nc.dram_tensor("m_scr", [D, T], F32, kind="ExternalInput").ap()
    io["ffn_w1"] = nc.dram_tensor("ffn_w1", [4096, D], F32, kind="ExternalInput").ap()
    io["ffn_b1"] = nc.dram_tensor("ffn_b1", [4096], F32, kind="ExternalInput").ap()
    io["ffn_w2"] = nc.dram_tensor("ffn_w2", [D, 4096], F32, kind="ExternalInput").ap()
    io["ffn_b2"] = nc.dram_tensor("ffn_b2", [D], F32, kind="ExternalInput").ap()
    io["x0s"] = nc.dram_tensor("x0s", [512, D], F32, kind="ExternalInput").ap()
    io["ews"] = nc.dram_tensor("ews", [512, 2], F32, kind="ExternalInput").ap()
    io["moe_w1"] = nc.dram_tensor("moe_w1", [D, 8192], F32, kind="ExternalInput").ap()
    io["moe_w2"] = nc.dram_tensor("moe_w2", [8192, D], F32, kind="ExternalInput").ap()
    io["ws_scr"] = nc.dram_tensor("ws_scr", [4, 128], F32, kind="Internal").ap()
    io["c_scr"] = nc.dram_tensor("c_scr", [T, D], F32, kind="ExternalOutput").ap()
    io["stats_scr"] = nc.dram_tensor("stats_scr", [1, 2], F32, kind="ExternalOutput").ap()
    io["moe_out"] = nc.dram_tensor("moe_out", [512, D], F32, kind="ExternalOutput").ap()
    plans = []
    with ExitStack() as ctx:
        p = Prog(nc, ctx)
        plans.append(_phase(nc, p, emit_a2, io))
        plans.append(_phase(nc, p, emit_a3, io))
    check_plans(plans)
    return nc


def build_c():
    nc = bass.Bass("TRN2", target_bir_lowering=False)
    io = {}
    io["x1_scr"] = nc.dram_tensor("x1_scr", [T, D], F32, kind="ExternalInput").ap()
    for nm, shape in [("qkv_w", [D, 3 * D]), ("qkv_b", [3 * D]),
                      ("w1_write", [D, 64]), ("w2_write", [D, 64]),
                      ("w1_read", [D, 64]), ("w2_read", [D, 64]),
                      ("memv_w", [D, D]), ("memv_b", [D]),
                      ("memg_w", [D, NHEAD]), ("memg_b", [NHEAD]),
                      ("mem_scale", [NHEAD]), ("out_w", [D, D]), ("out_b", [D]),
                      ("J", [6, 6]), ("decays", [NHEAD])]:
        io[nm] = nc.dram_tensor(nm, shape, F32, kind="ExternalInput").ap()
    for nm, shape, dt in [("den_scr", [1, TT], F32), ("g_scr", [4, 128], BF16),
                          ("s_scr", [21, NHEAD], F32), ("mem_scr", [2, NHEAD], F32)]:
        io[nm] = nc.dram_tensor(nm, shape, dt, kind="Internal").ap()
    io["ap_scr"] = nc.dram_tensor("ap_scr", [D, T], F32, kind="ExternalOutput").ap()
    plans = []
    import os
    npass = int(os.environ.get("C_NPASS", NPASS))
    with ExitStack() as ctx:
        p = Prog(nc, ctx)
        for hp in range(npass):
            plans.append(_phase(nc, p, lambda pp, pl, sb, ps_, ioo, hp=hp:
                                emit_c_pass(pp, pl, sb, ps_, ioo, hp), io))
    check_plans(plans)
    return nc


def c_in_map(inputs, x1):
    return dict(
        x1_scr=x1,
        qkv_w=inputs["L1_attn_qkv_w"], qkv_b=inputs["L1_attn_qkv_b"],
        w1_write=inputs["L1_attn_w1_write"], w2_write=inputs["L1_attn_w2_write"],
        w1_read=inputs["L1_attn_w1_read"], w2_read=inputs["L1_attn_w2_read"],
        memv_w=inputs["L1_attn_memv_w"], memv_b=inputs["L1_attn_memv_b"],
        memg_w=inputs["L1_attn_memg_w"], memg_b=inputs["L1_attn_memg_b"],
        mem_scale=inputs["L1_attn_mem_scale"], out_w=inputs["L1_attn_out_w"],
        out_b=inputs["L1_attn_out_b"], J=inputs["L1_attn_J"],
        decays=inputs["L1_attn_decays"],
    )


def build_b():
    nc = bass.Bass("TRN2", target_bir_lowering=False)
    io = {}
    io["x"] = nc.dram_tensor("x", [T, D], F32, kind="ExternalInput").ap()
    io["c_scr"] = nc.dram_tensor("c_scr", [T, D], F32, kind="ExternalInput").ap()
    io["stats_scr"] = nc.dram_tensor("stats_scr", [1, 2], F32, kind="ExternalInput").ap()
    io["moe_full"] = nc.dram_tensor("moe_full", [T, D], F32, kind="ExternalInput").ap()
    io["msk"] = nc.dram_tensor("msk", [1], F32, kind="ExternalInput").ap()
    io["mr_scr"] = nc.dram_tensor("mr_scr", [1, 2], F32, kind="Internal").ap()
    io["x1_scr"] = nc.dram_tensor("x1_scr", [T, D], F32, kind="ExternalOutput").ap()
    plans = []
    with ExitStack() as ctx:
        p = Prog(nc, ctx)
        plans.append(_phase(nc, p, emit_b, io))
    check_plans(plans)
    return nc


def a2a3_in_map(inputs, core):
    return dict(
        m_scr=None,
        ffn_w1=inputs["L0_attn_ffn_w1"], ffn_b1=inputs["L0_attn_ffn_b1"],
        ffn_w2=inputs["L0_attn_ffn_w2"], ffn_b2=inputs["L0_attn_ffn_b2"],
        x0s=np.ascontiguousarray(inputs["L0_attn_seq_embeddings"][0, core * 512:(core + 1) * 512]),
        ews=np.ascontiguousarray(inputs["L0_ffn_expert_weights"][core * 512:(core + 1) * 512]),
        moe_w1=inputs["L0_ffn_w1"], moe_w2=inputs["L0_ffn_w2"],
    )


def a1_in_map(inputs, core):
    x = np.ascontiguousarray(inputs["L0_attn_seq_embeddings"][core])
    return dict(
        x=x,
        in_proj_w=inputs["L0_attn_in_proj_w"],
        conv_w=inputs["L0_attn_conv_w"],
        conv_b=inputs["L0_attn_conv_b"],
        x_proj_w=inputs["L0_attn_x_proj_w"],
        dt_proj_w=inputs["L0_attn_dt_proj_w"],
        dt_proj_b=inputs["L0_attn_dt_proj_b"],
        A_log=inputs["L0_attn_A_log"],
        Dparam=inputs["L0_attn_D"],
        out_proj_w=inputs["L0_attn_out_proj_w"],
    )


def kernel(**inputs):
    inputs = {k: np.asarray(v) for k, v in inputs.items()}
    n = NCORES
    cores = list(range(n))

    nc1 = build_a1_test(debug=False)
    ims1 = [a1_in_map(inputs, c) for c in cores]
    r1 = run_bass_kernel_spmd(nc1, ims1, core_ids=cores).results

    nc2 = build_a2a3()
    ims2 = []
    for c in cores:
        im = a2a3_in_map(inputs, c)
        im["m_scr"] = r1[c]["m_scr"]
        ims2.append(im)
    r2 = run_bass_kernel_spmd(nc2, ims2, core_ids=cores).results
    moe_full = np.concatenate([r2[c]["moe_out"] for c in cores], axis=0)

    nc3 = build_b()
    ims3 = [dict(x=np.ascontiguousarray(inputs["L0_attn_seq_embeddings"][c]),
                 c_scr=r2[c]["c_scr"], stats_scr=r2[c]["stats_scr"], moe_full=moe_full,
                 msk=np.array([1.0 if c == 0 else 0.0], np.float32)) for c in cores]
    r3 = run_bass_kernel_spmd(nc3, ims3, core_ids=cores).results

    nc4 = build_c()
    ims4 = [c_in_map(inputs, r3[c]["x1_scr"]) for c in cores]
    r4 = run_bass_kernel_spmd(nc4, ims4, core_ids=cores).results

    nc5 = build_e()
    ims5 = [dict(x1_scr=r3[c]["x1_scr"], ap_scr=r4[c]["ap_scr"],
                 W1=inputs["L1_ffn_W1"], b1=inputs["L1_ffn_b1"],
                 W2=inputs["L1_ffn_W2"], b2=inputs["L1_ffn_b2"],
                 scale=np.asarray(inputs["L1_ffn_scale"]).reshape(1)) for c in cores]
    r5 = run_bass_kernel_spmd(nc5, ims5, core_ids=cores).results

    return np.stack([r5[c]["out"] for c in cores], axis=0)
